# version 10
# baseline (speedup 1.0000x reference)
import sys
sys.path.insert(0, '/opt/trn_rl_repo')
import contextlib
import numpy as np

import concourse.bass as bass
import concourse.bass_utils as _bass_utils
import concourse.mybir as mybir
from concourse.bass_utils import run_bass_kernel_spmd

if not getattr(_bass_utils, "_ldw_opt_patched", False):
    _orig_run_command = _bass_utils.run_command

    def _run_command_ldw(cmd, *a, **kw):
        if isinstance(cmd, list):
            cmd = ["--enable-ldw-opt=true" if c == "--enable-ldw-opt=false" else c
                   for c in cmd]
        return _orig_run_command(cmd, *a, **kw)

    _bass_utils.run_command = _run_command_ldw
    _bass_utils._ldw_opt_patched = True

F32 = mybir.dt.float32
BF16 = mybir.dt.bfloat16
AF = mybir.ActivationFunctionType
ALU = mybir.AluOpType
AX = mybir.AxisListType

D = 1024
S = 2048
C = 8
NTT = 4
TW = 512
H = 16
HD = 64
HID = 2730
HIDP = 2816
HC = 22
MH = 4096
K2 = 32
NE = 9
NEX = 8
EPS = 1e-5
LN8 = float(np.log(8.0))

_PCOLS = [("n1w", 8), ("n1b", 8), ("n2w", 8), ("n2b", 8), ("gnw", 8), ("gnb", 8),
          ("bq", 8), ("bk", 8), ("bg", 8), ("bo", 8), ("bout", 8),
          ("binu", HC), ("bing", HC), ("lng", 8), ("nlng", 8)]
PCOL_OFF = {}
_o = 0
for _n, _k in _PCOLS:
    PCOL_OFF[_n] = _o
    _o += _k
PCOLS_W = _o


class Sem:
    def __init__(self, handle):
        self.h = handle
        self.n = 0


class Rec:
    ENGINES = ("tensor", "vector", "scalar", "gpsimd", "sync")

    def __init__(self, nc):
        self.nc = nc
        self.ev = {e: [] for e in self.ENGINES}

    def op(self, eng, emitfn, waits=(), inc=None, inc_k=1):
        if inc is not None:
            inc.n += inc_k
        waits = tuple((s.h, v) for (s, v) in waits if v > 0)
        ih = None if inc is None else inc.h

        def f(e, emitfn=emitfn, waits=waits, ih=ih, inc_k=inc_k):
            for (h, v) in waits:
                e.wait_ge(h, v)
            ins = emitfn(e)
            if ih is not None:
                ins.then_inc(ih, inc_k)
        self.ev[eng].append(f)

    def wait(self, eng, sem, v):
        if v > 0:
            self.ev[eng].append(lambda e, h=sem.h, v=v: e.wait_ge(h, v))

    def flush(self, scope=None):
        ev = self.ev
        self.ev = {e: [] for e in self.ENGINES}
        import contextlib as _cl
        cm = self.nc.named_scope(scope) if scope else _cl.nullcontext()
        with cm:
            with self.nc.Block() as block:
                for name, reg in (("tensor", block.tensor), ("vector", block.vector),
                                  ("scalar", block.scalar), ("gpsimd", block.gpsimd),
                                  ("sync", block.sync)):
                    if ev[name]:
                        reg(lambda e, fs=tuple(ev[name]): [f(e) for f in fs])


NE_RUN = None

def build_program(debug_taps=()):
    nc = bass.Bass("TRN2", target_bir_lowering=False)
    ctx = contextlib.ExitStack()

    di = lambda name, shape: nc.dram_tensor(name, shape, F32, kind="ExternalInput")
    x_in = di("x_in", [D, S])
    Wq = di("Wq", [D, D]); Wk = di("Wk", [D, D]); Wv = di("Wv", [D, D])
    Wg = di("Wg", [D, D]); Wo = di("Wo", [D, D])
    winU = di("winU", [HC, 128, C, 128])
    winG = di("winG", [HC, 128, C, 128])
    woutP = di("woutP", [C, 128, HC, 128])
    exp1 = di("exp1", [NE, K2, 128, C, 128])
    exp2 = di("exp2", [NE, C, 128, K2, 128])
    gateWT_in = di("gateWT", [D, NEX])
    pcols_in = di("pcols_in", [128, PCOLS_W])
    masks_in = di("masks_in", [128, 4 * TW])
    R2_in = di("R2_in", [128, TW])
    ident_in = di("ident_in", [128, 128])
    esel_in = di("esel_in", [NEX, NEX * 128])
    sel2_in = di("sel2_in", [2, 128])
    selT_in = di("selT_in", [128, 2])
    bv_in = di("bv_row", [1, D])

    out_d = nc.dram_tensor("out", [D, S], F32, kind="ExternalOutput")

    dsc = lambda name, dt: nc.dram_tensor(name, [D, S], dt, kind="Internal")
    q_d = dsc("q_d", BF16)
    gate_d = dsc("gate_d", BF16)
    s_d = dsc("s_d", F32)
    woutbf_d = nc.dram_tensor("woutbf_d", [C, 128, HC * 128], BF16, kind="Internal")
    exp1bf_d = nc.dram_tensor("exp1bf_d", [NE, K2, 128, C * 128], BF16, kind="Internal")
    exp2bf_d = nc.dram_tensor("exp2bf_d", [NE, C, 128, K2 * 128], BF16, kind="Internal")
    x3_d = dsc("x3_d", F32)
    x3bf_d = dsc("x3bf_d", BF16)
    acc_d = dsc("acc_d", F32)

    tap_outs = []
    gfm_tap = None
    ge_tap = None
    for t in debug_taps:
        if t == "ge":
            ge_tap = nc.dram_tensor("tap_ge", [128, S], F32, kind="ExternalOutput")
            continue
        if t == "gfm":
            gfm_tap = nc.dram_tensor("tap_gfm", [NEX, S], BF16, kind="ExternalOutput")
            continue
        src = {"xn": xn_d, "q": q_d, "gate": gate_d, "s": s_d, "x3": x3_d,
               "x3bf": x3bf_d, "acc": acc_d}[t]
        dt = BF16 if t in ("xn", "q", "gate", "x3bf") else F32
        tap_outs.append((nc.dram_tensor("tap_" + t, [D, S], dt, kind="ExternalOutput"), src))

    def dview(d):
        return d[:, :].rearrange("(c p) t -> p c t", p=128)

    sems = {}
    def mksem(name):
        sems[name] = Sem(ctx.enter_context(nc.semaphore(name)))
        return sems[name]
    sPE = mksem("sPE"); sACT = mksem("sACT"); sDVE = mksem("sDVE")
    sLX = mksem("sLX")
    sLY = mksem("sLY")
    sLZ = mksem("sLZ")
    sWA = mksem("sWA")
    sW1 = mksem("sW1"); sW2 = mksem("sW2")
    sST = mksem("sST")
    ALL_DMA = (sLX, sLY, sLZ, sWA, sW1, sW2, sST)

    sb_root = lambda name, shape, dt: ctx.enter_context(nc.sbuf_tensor(name, shape, dt))
    pcols = sb_root("pcols", [128, PCOLS_W], F32)
    esel_bf = sb_root("esel_bf", [NEX, NEX * 128], BF16)
    g_ones = sb_root("g_ones", [1, S], BF16)
    g_fm = sb_root("g_fm", [NEX, S], BF16)
    g_tm = sb_root("g_tm", [128, NEX], F32)
    g_t1 = sb_root("g_t1", [128, NEX], F32)
    g_t2 = sb_root("g_t2", [128, NEX], F32)
    ones_col = sb_root("ones_col", [128, 1], BF16)
    ones_row = sb_root("ones_row", [1, 128], BF16)
    eps_col = sb_root("eps_col", [128, 1], F32)
    cc = sb_root("cc", [128, 8], F32)
    r_mu = sb_root("r_mu", [2, TW], F32)
    r_var = sb_root("r_var", [2, TW], F32)
    r_r = sb_root("r_r", [2, TW], F32)
    r_u = sb_root("r_u", [2, TW], F32)
    r_rbf = sb_root("r_rbf", [2, TW], BF16)
    r_ubf = sb_root("r_ubf", [2, TW], BF16)
    tmp = [sb_root(f"tmp{i}", [128, TW], F32) for i in range(4)]
    Ge_sb = sb_root("Ge_sb", [128, TW], F32)

    P = [ctx.enter_context(nc.psum_tensor(f"p{i}", [128, 512], F32)) for i in range(8)]

    rec = Rec(nc)
    def col(name, i=0, r0=0, r1=128):
        off = PCOL_OFF[name] + i
        return pcols[r0:r1, off:off + 1]

    def quiesce(scope=None):
        for eng, ss in (("gpsimd", (sWA, sW1, sW2, sST)), ("sync", (sLX, sLY, sLZ))):
            for s in ss:
                rec.wait(eng, s, s.n)
        rec.flush(scope)

    def ph0(g):
        for s in sems.values():
            g.sem_clear(s.h)
        g.memset(ones_col[:, :], 1.0)
        g.memset(ones_row[:, :], 1.0)
        g.memset(g_ones[:, :], 1.0)
        g.memset(eps_col[:, :], EPS)
    rec.ev["gpsimd"].append(ph0)
    rec.flush()
    rec.op("gpsimd", lambda e: e.dma_start(esel_bf[:, :], esel_in[:, :]),
           inc=sWA, inc_k=16)
    rec.op("sync", lambda e: e.dma_start(pcols[:, :], pcols_in[:, :]),
           inc=sLX, inc_k=16)
    quiesce("init1")

    regAW = ctx.enter_context(contextlib.ExitStack())
    sbAW = lambda name, shape, dt: regAW.enter_context(nc.sbuf_tensor(name, shape, dt))
    masks_bf = sbAW("masks_bf", [128, 4 * TW], BF16)
    R2 = sbAW("R2", [128, TW], F32)
    qsc = [sbAW(f"qsc{i}", [128, TW], F32) for i in range(2)]
    sel2_bf = sbAW("sel2_bf", [2, 128], BF16)
    selT_bf = sbAW("selT_bf", [128, 2], BF16)
    bv_bf = sbAW("bv_bf", [1, D], BF16)
    gateWT_bf = sbAW("gateWT_bf", [128, C * NEX], BF16)
    ident = sbAW("ident", [128, 128], F32)

    rec.op("gpsimd", lambda e: e.dma_start(masks_bf[:, :], masks_in[:, :]),
           inc=sWA, inc_k=16)
    rec.op("gpsimd", lambda e: e.dma_start(bv_bf[:, :], bv_in[:, :]),
           inc=sWA, inc_k=16)
    rec.op("gpsimd", lambda e: e.dma_start(
        gateWT_bf[:, :].rearrange("p (c e) -> p c e", e=NEX),
        gateWT_in[:, :].rearrange("(c p) e -> p c e", p=128)),
        inc=sWA, inc_k=16)
    rec.op("gpsimd", lambda e: e.dma_start(sel2_bf[:, :], sel2_in[:, :]), inc=sWA, inc_k=16)
    rec.op("gpsimd", lambda e: e.dma_start(selT_bf[:, :], selT_in[:, :]), inc=sWA, inc_k=16)
    rec.op("sync", lambda e: e.dma_start(R2[:, :], R2_in[:, :]), inc=sLX, inc_k=16)
    rec.op("sync", lambda e: e.dma_start(ident[:, :], ident_in[:, :]), inc=sLX, inc_k=16)
    quiesce("init2")

    def vop(emit, extra=(), engine="vector", sem=None):
        sem = sem or (sDVE if engine == "vector" else sACT)
        prev = sem.n
        rec.op(engine, emit, waits=list(extra) + [(sem, prev)], inc=sem)
        return sem.n

    def ln_rows(inv_n, pA, pB, vstat, nr=1):
        vop(lambda e: e.tensor_scalar_mul(r_mu[0:nr, :], pA[0:nr, 0:TW], inv_n),
            extra=[(sPE, vstat)])
        vop(lambda e: e.tensor_scalar_mul(r_var[0:nr, :], pB[0:nr, 0:TW], inv_n))
        vop(lambda e: e.tensor_tensor(r_u[0:nr, :], r_mu[0:nr, :], r_mu[0:nr, :], ALU.mult))
        vv = vop(lambda e: e.tensor_tensor(r_var[0:nr, :], r_var[0:nr, :], r_u[0:nr, :], ALU.subtract))
        vop(lambda e: e.activation(r_var[0:nr, :], r_var[0:nr, :], AF.Ln,
                                   bias=eps_col[0:nr, 0:1]),
            extra=[(sDVE, vv)], engine="scalar")
        va = vop(lambda e: e.activation(r_r[0:nr, :], r_var[0:nr, :], AF.Exp, scale=-0.5),
                 engine="scalar")
        vop(lambda e: e.tensor_tensor(r_u[0:nr, :], r_mu[0:nr, :], r_r[0:nr, :], ALU.mult),
            extra=[(sACT, va)])
        vop(lambda e: e.tensor_copy(r_rbf[0:nr, :], r_r[0:nr, :]))
        vop(lambda e: e.tensor_copy(r_ubf[0:nr, :], r_u[0:nr, :]))
        return sDVE.n

    def ln_tile(src_fn, aff, writers, x_bf, sq_bf, src_ready=(), n_feat=D):
        for c in range(C):
            w = list(src_ready) if c == 0 else []
            rec.op("vector", lambda e, c=c: e.tensor_copy(x_bf[:, c * TW:(c + 1) * TW], src_fn(c)),
                   waits=w, inc=(sDVE if c == C - 1 else None))
        vcast = sDVE.n
        for c in range(C):
            rec.op("scalar", lambda e, c=c: e.activation(sq_bf[:, c * TW:(c + 1) * TW],
                                                         x_bf[:, c * TW:(c + 1) * TW], AF.Square),
                   waits=[(sDVE, vcast)] if c == 0 else (), inc=(sACT if c == C - 1 else None))
        vsq = sACT.n
        for c in range(C):
            rec.op("tensor", lambda e, c=c: e.matmul(P[2][0:1, 0:TW], ones_col[:, 0:1],
                                                     x_bf[:, c * TW:(c + 1) * TW],
                                                     start=(c == 0), stop=(c == C - 1)),
                   waits=[(sDVE, vcast)] if c == 0 else ())
        for c in range(C):
            rec.op("tensor", lambda e, c=c: e.matmul(P[3][0:1, 0:TW], ones_col[:, 0:1],
                                                     sq_bf[:, c * TW:(c + 1) * TW],
                                                     start=(c == 0), stop=(c == C - 1)),
                   waits=[(sACT, vsq)] if c == 0 else (),
                   inc=(sPE if c == C - 1 else None))
        vrows = ln_rows(1.0 / n_feat, P[2], P[3], sPE.n)
        rec.op("tensor", lambda e: e.matmul(P[4][:, 0:TW], ones_row[0:1, :], r_rbf[0:1, :],
                                            start=True, stop=True),
               waits=[(sDVE, vrows)])
        rec.op("tensor", lambda e: e.matmul(P[5][:, 0:TW], ones_row[0:1, :], r_ubf[0:1, :],
                                            start=True, stop=True), inc=sPE)
        vbc = sPE.n
        for c in range(C):
            tm = tmp[c % 2]
            rec.op("vector", lambda e, c=c, tm=tm: e.tensor_tensor(tm[:, :], src_fn(c),
                                                                   P[4][:, 0:TW], ALU.mult),
                   waits=[(sPE, vbc)] if c == 0 else ())
            rec.op("vector", lambda e, tm=tm: e.tensor_tensor(tm[:, :], tm[:, :],
                                                              P[5][:, 0:TW], ALU.subtract))
            for wi, (dst_fn, mode) in enumerate(writers):
                last = (c == C - 1 and wi == len(writers) - 1)
                if mode == "aff":
                    wname, bname = aff
                    rec.op("vector", lambda e, c=c, tm=tm, dst_fn=dst_fn, wname=wname, bname=bname:
                           e.tensor_scalar(dst_fn(c), tm[:, :], col(wname, c), col(bname, c),
                                           ALU.mult, ALU.add),
                           inc=(sDVE if last else None))
                else:
                    rec.op("vector", lambda e, c=c, tm=tm, dst_fn=dst_fn:
                           e.tensor_copy(dst_fn(c), tm[:, :]),
                           inc=(sDVE if last else None))
        return sDVE.n

    ev_tmp = [0, 0]
    ev_router_act = [0]

    cast_w1_done = [0] * NE
    cast_w2_done = [0] * NE
    _cast_q = []
    for E in range(NE):
        for half in range(2):
            _cast_q.append(("w1", E, half))
        for half in range(2):
            _cast_q.append(("w2", E, half))

    def drip_cast(n=1):
        for _ in range(n):
            if not _cast_q:
                return
            kind, E, half = _cast_q.pop(0)
            if kind == "w1":
                rec.op("gpsimd", lambda e, E=E, half=half: e.dma_start(
                    exp1bf_d[E, half * 16:(half + 1) * 16],
                    exp1[E, half * 16:(half + 1) * 16].rearrange("a p k m -> a p (k m)")),
                    inc=sST, inc_k=16)
                if half == 1:
                    cast_w1_done[E] = sST.n
            else:
                rec.op("gpsimd", lambda e, E=E, half=half: e.dma_start(
                    exp2bf_d[E, half * 4:(half + 1) * 4],
                    exp2[E, half * 4:(half + 1) * 4].rearrange("a p k m -> a p (k m)")),
                    inc=sST, inc_k=16)
                if half == 1:
                    cast_w2_done[E] = sST.n

    regSH = ctx.enter_context(contextlib.ExitStack())
    sbSH = lambda name, shape, dt: regSH.enter_context(nc.sbuf_tensor(name, shape, dt))
    k_sb = sbSH("k_sb", [128, C * S], BF16)
    v_sb = sbSH("v_sb", [128, 16 * D], BF16)
    gate_t = sbSH("gate_t", [128, C * TW], BF16)
    wo_sb = sbSH("wo_sb", [128, C * D], BF16)
    q_t = sbSH("q_t", [128, C * TW], BF16)

    regA12 = contextlib.ExitStack()
    sbA12 = lambda name, shape, dt: regA12.enter_context(nc.sbuf_tensor(name, shape, dt))
    xn_sb = sbA12("xn_sb", [128, C * S], BF16)

    x_war = [0, 0]
    regA12a = contextlib.ExitStack()
    sbA12a = lambda name, shape, dt: regA12a.enter_context(nc.sbuf_tensor(name, shape, dt))
    x_t = [sbA12a("x_t0", [128, C * TW], F32)]
    xbf_t = sbA12a("xbf_t", [128, C * TW], BF16)
    sqbf_t = sbA12a("sqbf_t", [128, C * TW], BF16)

    xv = dview(x_in)
    for tt in range(NTT):
        xt = x_t[0]
        rec.op("sync", lambda e, tt=tt, xt=xt: e.dma_start(
            xt[:, :].rearrange("p (c n) -> p c n", n=TW), xv[:, :, tt * TW:(tt + 1) * TW]),
            waits=[(sDVE, x_war[0])], inc=sLX, inc_k=16)
        vload = sLX.n
        vdone = ln_tile(lambda c, xt=xt: xt[:, c * TW:(c + 1) * TW], ("n1w", "n1b"),
                        [(lambda c, tt=tt: xn_sb[:, c * S + tt * TW: c * S + (tt + 1) * TW], "aff")],
                        xbf_t, sqbf_t, src_ready=[(sLX, vload)])
        ev_tmp[0] = ev_tmp[1] = vdone
        x_war[0] = vdone
    v_ln1 = sDVE.n
    quiesce("A12a")
    regA12a.close()

    regA12b = contextlib.ExitStack()
    sbA12b = lambda name, shape, dt: regA12b.enter_context(nc.sbuf_tensor(name, shape, dt))
    wbuf = [sbA12b(f"wbuf{i}", [128, C * D], BF16) for i in range(2)]

    qv = dview(q_d)
    gv = dview(gate_d)
    st_gate = [0] * NTT

    def proj_weight_load(W, buf, after_pe):
        rec.op("gpsimd", lambda e, W=W, buf=buf: e.dma_start(
            buf[:, :].rearrange("p (k m) -> p k m", m=D),
            W[:, :].rearrange("(k p) m -> p k m", p=128)),
            waits=[(sPE, after_pe)], inc=sWA, inc_k=16)
        return sWA.n

    wbuf_war = [0, 0]
    ev_qsc = [0, 0]
    ev_cc = [0, 0]
    psum_war = [0, 0]

    projs = (("q", Wq), ("k", Wk), ("v", Wv), ("g", Wg))
    vw_all = []
    for pi, (proj, W) in enumerate(projs[:2]):
        vw_all.append(proj_weight_load(W, wbuf[pi % 2], wbuf_war[pi % 2]))
    for pi, (proj, W) in enumerate(projs):
        vw = vw_all[pi]
        wb = wbuf[pi % 2]
        for tt in range(NTT):
            vxn = v_ln1
            xn_l = xn_sb[:, :]
            if proj in ("q", "k"):
                for oc in range(C):
                    i = oc % 2
                    pb = P[6 + i]
                    lncol = "lng" if proj == "q" else "nlng"
                    bias2 = -LN8 if proj == "q" else 0.0
                    rec.op("vector", lambda e, oc=oc, i=i, lncol=lncol, bias2=bias2, tt=tt:
                           e.tensor_scalar(cc[:, i:i + 1], col(lncol, oc), float(tt * TW), bias2,
                                           ALU.mult, ALU.add),
                           waits=[(sACT, ev_cc[i])], inc=sDVE)
                    vb = sDVE.n
                    rec.op("scalar", lambda e, oc=oc, i=i, lncol=lncol:
                           e.activation(qsc[i][:, :], R2[:, :], AF.Exp,
                                        bias=cc[:, i:i + 1], scale=col(lncol, oc)),
                           waits=[(sDVE, max(vb, ev_qsc[i]))], inc=sACT)
                    ev_cc[i] = sACT.n
                    vgen = sACT.n
                    for c in range(C):
                        w = [(sDVE, vxn), (sWA, vw), (sACT, psum_war[i])] if c == 0 else []
                        rec.op("tensor", lambda e, c=c, oc=oc, tt=tt, pb=pb, wb=wb:
                               e.matmul(pb[:, 0:TW], wb[:, c * D + oc * 128: c * D + oc * 128 + 128],
                                        xn_sb[:, c * S + tt * TW: c * S + (tt + 1) * TW],
                                        start=(c == 0), stop=(c == C - 1)),
                               waits=w, inc=(sPE if c == C - 1 else None))
                    vmm = sPE.n
                    bcol = "bq" if proj == "q" else "bk"
                    rec.op("scalar", lambda e, pb=pb, i=i, oc=oc, bcol=bcol:
                           e.activation(tmp[i][:, :], pb[:, 0:TW], AF.Identity, bias=col(bcol, oc)),
                           waits=[(sPE, vmm), (sDVE, ev_tmp[i])], inc=sACT)
                    psum_war[i] = sACT.n
                    vev = sACT.n
                    dst_sb = q_t if proj == "q" else k_sb
                    if proj == "q":
                        dsl = lambda e, oc=oc: q_t[:, oc * TW:(oc + 1) * TW]
                    else:
                        dsl = lambda e, oc=oc, tt=tt: k_sb[:, oc * S + tt * TW: oc * S + (tt + 1) * TW]
                    rec.op("vector", lambda e, i=i, dsl=dsl:
                           e.tensor_tensor(dsl(e), tmp[i][:, :], qsc[i][:, :], ALU.mult),
                           waits=[(sACT, vgen), (sACT, vev)], inc=sDVE)
                    ev_tmp[i] = sDVE.n
                    ev_qsc[i] = sDVE.n
                if proj == "q":
                    rec.op("gpsimd", lambda e, tt=tt: e.dma_start(
                        qv[:, :, tt * TW:(tt + 1) * TW],
                        q_t[:, :].rearrange("p (c n) -> p c n", n=TW)),
                        waits=[(sDVE, sDVE.n)], inc=sST, inc_k=16)
                    rec.wait("vector", sST, sST.n)
            elif proj == "v":
                for grp in range(4):
                    gg_ = tt * 4 + grp
                    for half in range(2):
                        i = (grp * 2 + half) % 2
                        pb = P[6 + i]
                        for c in range(C):
                            w = [(sDVE, vxn), (sWA, vw), (sACT, psum_war[i])] if c == 0 else []
                            rec.op("tensor", lambda e, c=c, grp=grp, half=half, tt=tt, pb=pb, wb=wb:
                                   e.matmul(pb[:, 0:TW],
                                            xn_sb[:, c * S + tt * TW + grp * 128: c * S + tt * TW + grp * 128 + 128],
                                            wb[:, c * D + half * 512: c * D + (half + 1) * 512],
                                            start=(c == 0), stop=False),
                                   waits=w)
                        rec.op("tensor", lambda e, half=half, pb=pb:
                               e.matmul(pb[:, 0:TW], ones_row[0:1, :],
                                        bv_bf[0:1, half * 512:(half + 1) * 512],
                                        start=False, stop=True), inc=sPE)
                        vmm = sPE.n
                        rec.op("scalar", lambda e, gg_=gg_, half=half, pb=pb:
                               e.activation(v_sb[:, gg_ * D + half * 512: gg_ * D + (half + 1) * 512],
                                            pb[:, 0:TW], AF.Copy),
                               waits=[(sPE, vmm)], inc=sACT)
                        psum_war[i] = sACT.n
            else:
                for oc in range(C):
                    i = oc % 2
                    pb = P[6 + i]
                    for c in range(C):
                        w = [(sDVE, vxn), (sWA, vw), (sACT, psum_war[i])] if c == 0 else []
                        if c == 0 and tt > 0:
                            w.append((sST, st_gate[tt - 1]))
                        rec.op("tensor", lambda e, c=c, oc=oc, tt=tt, pb=pb, wb=wb:
                               e.matmul(pb[:, 0:TW], wb[:, c * D + oc * 128: c * D + oc * 128 + 128],
                                        xn_sb[:, c * S + tt * TW: c * S + (tt + 1) * TW],
                                        start=(c == 0), stop=(c == C - 1)),
                               waits=w, inc=(sPE if c == C - 1 else None))
                    vmm = sPE.n
                    rec.op("scalar", lambda e, oc=oc, pb=pb:
                           e.activation(gate_t[:, oc * TW:(oc + 1) * TW], pb[:, 0:TW],
                                        AF.Silu, bias=col("bg", oc)),
                           waits=[(sPE, vmm)], inc=sACT)
                    psum_war[i] = sACT.n
                rec.op("gpsimd", lambda e, tt=tt: e.dma_start(
                    gv[:, :, tt * TW:(tt + 1) * TW],
                    gate_t[:, :].rearrange("p (c n) -> p c n", n=TW)),
                    waits=[(sACT, sACT.n)], inc=sST, inc_k=16)
                st_gate[tt] = sST.n
        drip_cast(2)
        wbuf_war[pi % 2] = sPE.n
        if pi + 2 < len(projs):
            vw_all.append(proj_weight_load(projs[pi + 2][1], wbuf[pi % 2], wbuf_war[pi % 2]))

    rec.op("gpsimd", lambda e: e.dma_start(
        wo_sb[:, :].rearrange("p (k m) -> p k m", m=D),
        Wo[:, :].rearrange("(k p) m -> p k m", p=128)),
        inc=sWA, inc_k=16)
    vwo = sWA.n
    quiesce("A12")
    regA12b.close()
    regA12.close()

    regA3 = contextlib.ExitStack()
    sbA3 = lambda name, shape, dt: regA3.enter_context(nc.sbuf_tensor(name, shape, dt))
    s_t = sbA3("s_t", [128, C * TW], F32)
    xorig_t = sbA3("xorig_t", [128, C * TW], F32)
    gg_sb = sbA3("gg_sb", [128, C * TW], BF16)
    sTa = sbA3("sTa", [128, 4 * TW], BF16)
    sTd = sbA3("sTd", [128, 4 * TW], BF16)
    ret_bf = sbA3("ret_bf", [128, TW], BF16)
    retsq_bf = sbA3("retsq_bf", [128, TW], BF16)

    sv = dview(s_d)
    warB = {0: (sACT, 0), 1: (sACT, 0), 3: (sACT, 0), 4: (sACT, 0)}
    war_slotA = [0, 0, 0, 0]
    war_slotD = [0, 0, 0, 0]
    ev_gg = 0
    st_s_val = 0
    v_apply_prev = 0
    for qt in range(NTT):
        rec.op("sync", lambda e, qt=qt: e.dma_start(
            gate_t[:, :].rearrange("p (c n) -> p c n", n=TW), gv[:, :, qt * TW:(qt + 1) * TW]),
            waits=[(sDVE, ev_gg), (sST, st_gate[NTT - 1])], inc=sLY, inc_k=16)
        vgate = sLY.n
        rec.op("sync", lambda e, qt=qt: e.dma_start(
            xorig_t[:, :].rearrange("p (c n) -> p c n", n=TW), xv[:, :, qt * TW:(qt + 1) * TW]),
            waits=[(sDVE, sDVE.n)], inc=sLY, inc_k=16)
        vxorig = sLY.n
        rec.op("sync", lambda e, qt=qt: e.dma_start(
            q_t[:, :].rearrange("p (c n) -> p c n", n=TW), qv[:, :, qt * TW:(qt + 1) * TW]),
            waits=[(sPE, sPE.n)], inc=sLY, inc_k=16)
        vq = sLY.n

        nkt = 4 * qt + 4
        for hp in range(H // 2):
            c = hp
            banks = [[0, 1], [3, 4]]
            na = [0, 0]
            nd = [0, 0]
            ev_info = [dict(), dict()]
            def emit_sc(hi, kt):
                rb = hi * 64
                h = 2 * hp + hi
                bid = banks[hi][kt % 2]
                pb = P[bid]
                ws, wv = warB[bid]
                w = [(ws, wv)]
                if kt == 0:
                    w.append((sLY, vq))
                rec.op("tensor", lambda e, c=c, rb=rb, kt=kt, pb=pb:
                       e.matmul(pb[:, 0:TW],
                                k_sb[rb:rb + 64, c * S + kt * 128: c * S + kt * 128 + 128],
                                q_t[rb:rb + 64, c * TW:(c + 1) * TW],
                                start=True, stop=True),
                       waits=w, inc=sPE)
                return sPE.n, bid

            def emit_evac(hi, kt, vsc, bid):
                pb = P[bid]
                masked = (kt >= 4 * qt)
                if not masked:
                    sl = hi * 2 + (na[hi] % 2)
                    if hi == 0:
                        rec.op("scalar", lambda e, pb=pb, sl=sl:
                               e.activation(sTa[:, sl * TW:(sl + 1) * TW], pb[:, 0:TW], AF.Copy),
                               waits=[(sPE, vsc), (sPE, war_slotA[sl])], inc=sACT)
                        warB[bid] = (sACT, sACT.n)
                        ev_info[hi][kt] = (sACT, sACT.n, "a", sl)
                    else:
                        rec.op("vector", lambda e, pb=pb, sl=sl:
                               e.tensor_copy(sTa[:, sl * TW:(sl + 1) * TW], pb[:, 0:TW]),
                               waits=[(sPE, vsc), (sPE, war_slotA[sl])], inc=sDVE)
                        warB[bid] = (sDVE, sDVE.n)
                        ev_info[hi][kt] = (sDVE, sDVE.n, "a", sl)
                    na[hi] += 1
                else:
                    j = kt - 4 * qt
                    sl = hi * 2 + (nd[hi] % 2)
                    rec.op("vector", lambda e, pb=pb, sl=sl, j=j:
                           e.tensor_tensor(sTd[:, sl * TW:(sl + 1) * TW], pb[:, 0:TW],
                                           masks_bf[:, j * TW:(j + 1) * TW], ALU.mult),
                           waits=[(sPE, vsc), (sPE, war_slotD[sl])], inc=sDVE)
                    warB[bid] = (sDVE, sDVE.n)
                    ev_info[hi][kt] = (sDVE, sDVE.n, "d", sl)
                    nd[hi] += 1

            def emit_ret(hi, kt):
                rb = hi * 64
                h = 2 * hp + hi
                evsem, vev, kind, sl = ev_info[hi][kt]
                buf = sTa if kind == "a" else sTd
                w = [(evsem, vev)]
                if kt == 0:
                    w.append((sDVE, v_apply_prev))
                rec.op("tensor", lambda e, kt=kt, h=h, rb=rb, buf=buf, sl=sl, nkt=nkt:
                       e.matmul(P[2][rb:rb + 64, 0:TW],
                                v_sb[:, kt * D + h * 64: kt * D + h * 64 + 64],
                                buf[:, sl * TW:(sl + 1) * TW],
                                start=(kt == 0), stop=(kt == nkt - 1)),
                       waits=w, inc=sPE)
                if kind == "a":
                    war_slotA[sl] = sPE.n
                else:
                    war_slotD[sl] = sPE.n
                return sPE.n

            vsc_bid = [None, None]
            for kt in range(nkt):
                for hi in range(2):
                    vsc_bid[hi] = emit_sc(hi, kt)
                for hi in range(2):
                    emit_evac(hi, kt, *vsc_bid[hi])
                if kt > 0:
                    for hi in range(2):
                        emit_ret(hi, kt - 1)
            for hi in range(2):
                vret_last = emit_ret(hi, nkt - 1)
            rec.op("scalar", lambda e: e.activation(ret_bf[:, :], P[2][:, 0:TW], AF.Copy),
                   waits=[(sPE, vret_last), (sDVE, v_apply_prev)])
            rec.op("scalar", lambda e: e.activation(retsq_bf[:, :], ret_bf[:, :], AF.Square),
                   inc=sACT)
            vrsq = sACT.n
            rec.op("tensor", lambda e: e.matmul(P[5][0:2, 0:TW], selT_bf[:, 0:2],
                                                ret_bf[:, :], start=True, stop=True),
                   waits=[(sACT, vrsq)])
            rec.op("tensor", lambda e: e.matmul(P[6][0:2, 0:TW], selT_bf[:, 0:2],
                                                retsq_bf[:, :], start=True, stop=True),
                   inc=sPE)
            vrows = ln_rows(1.0 / 64.0, P[5], P[6], sPE.n, nr=2)
            rec.op("tensor", lambda e: e.matmul(P[7][:, 0:TW], sel2_bf[0:2, :],
                                                r_rbf[0:2, :], start=True, stop=True),
                   waits=[(sDVE, vrows)])
            rec.op("tensor", lambda e: e.matmul(P[2][:, 0:TW], sel2_bf[0:2, :],
                                                r_ubf[0:2, :], start=True, stop=True),
                   inc=sPE)
            vbc = sPE.n
            tm = tmp[2]
            rec.op("vector", lambda e, tm=tm: e.tensor_tensor(
                tm[:, :], ret_bf[:, :], P[7][:, 0:TW], ALU.mult),
                waits=[(sPE, vbc)])
            rec.op("vector", lambda e, tm=tm: e.tensor_tensor(
                tm[:, :], tm[:, :], P[2][:, 0:TW], ALU.subtract))
            rec.op("vector", lambda e, tm=tm, c=c: e.tensor_scalar(
                tm[:, :], tm[:, :], col("gnw", c), col("gnb", c), ALU.mult, ALU.add))
            rec.op("vector", lambda e, tm=tm, c=c: e.tensor_tensor(
                gg_sb[:, c * TW:(c + 1) * TW], tm[:, :],
                gate_t[:, c * TW:(c + 1) * TW], ALU.mult),
                waits=[(sLY, vgate)], inc=sDVE)
            v_apply_prev = sDVE.n
            drip_cast(1)
        ev_gg = sDVE.n
        for oc in range(C):
            i = oc % 2
            pb = P[i]
            ws, wv = warB[i]
            for c in range(C):
                w = [(ws, wv), (sDVE, ev_gg), (sWA, vwo)] if c == 0 else []
                rec.op("tensor", lambda e, c=c, oc=oc, pb=pb:
                       e.matmul(pb[:, 0:TW], wo_sb[:, c * D + oc * 128: c * D + oc * 128 + 128],
                                gg_sb[:, c * TW:(c + 1) * TW],
                                start=(c == 0), stop=(c == C - 1)),
                       waits=w, inc=(sPE if c == C - 1 else None))
            vmm = sPE.n
            rec.op("scalar", lambda e, oc=oc, pb=pb, i=i:
                   e.activation(tmp[i][:, :], pb[:, 0:TW], AF.Identity, bias=col("bo", oc)),
                   waits=[(sPE, vmm), (sDVE, ev_tmp[i]), (sST, st_s_val)], inc=sACT)
            warB[i] = (sACT, sACT.n)
            vev = sACT.n
            rec.op("vector", lambda e, oc=oc, i=i: e.tensor_tensor(
                s_t[:, oc * TW:(oc + 1) * TW], tmp[i][:, :],
                xorig_t[:, oc * TW:(oc + 1) * TW], ALU.add),
                waits=[(sACT, vev), (sLY, vxorig)], inc=sDVE)
            ev_tmp[i] = sDVE.n
        rec.op("gpsimd", lambda e, qt=qt: e.dma_start(
            sv[:, :, qt * TW:(qt + 1) * TW], s_t[:, :].rearrange("p (c n) -> p c n", n=TW)),
            waits=[(sDVE, sDVE.n)], inc=sST, inc_k=16)
        st_s_val = sST.n
    quiesce("A3")
    regA3.close()
    regSH.close()

    regA4 = contextlib.ExitStack()
    sbA4 = lambda name, shape, dt: regA4.enter_context(nc.sbuf_tensor(name, shape, dt))
    hidden = sbA4("hidden", [128, HC * S], BF16)

    regA4a = contextlib.ExitStack()
    sbA4a = lambda name, shape, dt: regA4a.enter_context(nc.sbuf_tensor(name, shape, dt))
    x2_sb = sbA4a("x2_sb", [128, C * S], BF16)
    uA = [sbA4a(f"uA{i}", [128, C * 128], BF16) for i in range(2)]
    uG = [sbA4a(f"uG{i}", [128, C * 128], BF16) for i in range(2)]
    s4_t = sbA4a("s4_t", [128, C * TW], F32)
    xbf4 = sbA4a("xbf4", [128, C * TW], BF16)
    sqbf4 = sbA4a("sqbf4", [128, C * TW], BF16)

    for tt in range(NTT):
        rec.op("sync", lambda e, tt=tt: e.dma_start(
            s4_t[:, :].rearrange("p (c n) -> p c n", n=TW), sv[:, :, tt * TW:(tt + 1) * TW]),
            waits=[(sDVE, sDVE.n)], inc=sLY, inc_k=16)
        vload = sLY.n
        vx2done = ln_tile(lambda c: s4_t[:, c * TW:(c + 1) * TW], ("n2w", "n2b"),
                          [(lambda c, tt=tt: x2_sb[:, c * S + tt * TW: c * S + (tt + 1) * TW], "aff")],
                          xbf4, sqbf4, src_ready=[(sLY, vload)])
        ev_tmp[0] = ev_tmp[1] = vx2done

    for oc in range(C):
        rec.op("gpsimd", lambda e, oc=oc: e.dma_start(
            woutbf_d[oc].rearrange("p (h m) -> p h m", m=128), woutP[oc]),
            inc=sST, inc_k=16)
    st_woutbf = sST.n

    war_ring = [0, 0]
    war_pu = [0, 0, 0, 0]
    ev_tmpA = [0, 0, 0, 0]
    for hc in range(HC):
        i = hc % 2
        rec.op("gpsimd", lambda e, hc=hc, i=i: e.dma_start(
            uA[i][:, :].rearrange("p (k m) -> p k m", m=128), winU[hc]),
            waits=[(sPE, war_ring[i])], inc=sWA, inc_k=16)
        rec.op("gpsimd", lambda e, hc=hc, i=i: e.dma_start(
            uG[i][:, :].rearrange("p (k m) -> p k m", m=128), winG[hc]),
            inc=sWA, inc_k=16)
        vw = sWA.n
        for tt in range(NTT):
            j = tt % 2
            pu = P[0 + j]
            pg = P[2 + j]
            for c in range(C):
                w = [(sWA, vw), (sACT, war_pu[j]), (sDVE, vx2done)] if c == 0 else []
                rec.op("tensor", lambda e, c=c, tt=tt, i=i, pu=pu:
                       e.matmul(pu[:, 0:TW], uA[i][:, c * 128:(c + 1) * 128],
                                x2_sb[:, c * S + tt * TW: c * S + (tt + 1) * TW],
                                start=(c == 0), stop=(c == C - 1)),
                       waits=w)
            for c in range(C):
                w = [(sACT, war_pu[2 + j]), (sDVE, vx2done)] if c == 0 else []
                rec.op("tensor", lambda e, c=c, tt=tt, i=i, pg=pg:
                       e.matmul(pg[:, 0:TW], uG[i][:, c * 128:(c + 1) * 128],
                                x2_sb[:, c * S + tt * TW: c * S + (tt + 1) * TW],
                                start=(c == 0), stop=(c == C - 1)),
                       waits=w, inc=(sPE if c == C - 1 else None))
            vmm = sPE.n
            rec.op("scalar", lambda e, hc=hc, j=j, pu=pu:
                   e.activation(tmp[j][:, :], pu[:, 0:TW], AF.Silu, bias=col("binu", hc)),
                   waits=[(sPE, vmm), (sDVE, ev_tmpA[j])], inc=sACT)
            war_pu[j] = sACT.n
            rec.op("scalar", lambda e, hc=hc, j=j, pg=pg:
                   e.activation(tmp[2 + j][:, :], pg[:, 0:TW], AF.Identity, bias=col("bing", hc)),
                   waits=[(sDVE, ev_tmpA[2 + j])], inc=sACT)
            war_pu[2 + j] = sACT.n
            vev = sACT.n
            rec.op("vector", lambda e, hc=hc, tt=tt, j=j: e.tensor_tensor(
                hidden[:, hc * S + tt * TW: hc * S + (tt + 1) * TW],
                tmp[j][:, :], tmp[2 + j][:, :], ALU.mult),
                waits=[(sACT, vev)], inc=sDVE)
            ev_tmpA[j] = sDVE.n
            ev_tmpA[2 + j] = sDVE.n
        drip_cast(1)
        war_ring[i] = sPE.n
    quiesce("A4a")
    regA4a.close()

    regA4b = contextlib.ExitStack()
    sbA4b = lambda name, shape, dt: regA4b.enter_context(nc.sbuf_tensor(name, shape, dt))
    wring = [sbA4b(f"wring{i}", [128, HC * 128], BF16) for i in range(2)]
    s_t2 = sbA4b("s_t2", [128, C * TW], F32)
    xo_t2 = sbA4b("xo_t2", [128, C * TW], F32)
    x3bf_t = sbA4b("x3bf_t", [128, C * TW], BF16)
    xbf5 = sbA4b("xbf5", [128, C * TW], BF16)
    sqbf5 = sbA4b("sqbf5", [128, C * TW], BF16)

    x3v = dview(x3_d)
    x3bv = dview(x3bf_d)
    war_wring = [0, 0]
    war_po = [0, 0]
    war_rps = [0, 0]
    st_x3 = 0
    for tt in range(NTT):
        rec.op("sync", lambda e, tt=tt: e.dma_start(
            s_t2[:, :].rearrange("p (c n) -> p c n", n=TW), sv[:, :, tt * TW:(tt + 1) * TW]),
            waits=[(sDVE, sDVE.n), (sST, st_x3)], inc=sLY, inc_k=16)
        vs = sLY.n
        rec.op("sync", lambda e, tt=tt: e.dma_start(
            xo_t2[:, :].rearrange("p (c n) -> p c n", n=TW), xv[:, :, tt * TW:(tt + 1) * TW]),
            waits=[(sDVE, sDVE.n)], inc=sLY, inc_k=16)
        vxo = sLY.n
        for oc in range(C):
            i = oc % 2
            pb = P[4 + i]
            rec.op("sync", lambda e, oc=oc, i=i: e.dma_start(
                wring[i][:, :], woutbf_d[oc]),
                waits=[(sPE, war_wring[i]), (sST, st_woutbf)], inc=sLX, inc_k=16)
            vw = sLX.n
            for hc in range(HC):
                w = [(sLX, vw), (sACT, war_po[i])] if hc == 0 else []
                rec.op("tensor", lambda e, hc=hc, tt=tt, i=i, pb=pb:
                       e.matmul(pb[:, 0:TW], wring[i][:, hc * 128:(hc + 1) * 128],
                                hidden[:, hc * S + tt * TW: hc * S + (tt + 1) * TW],
                                start=(hc == 0), stop=(hc == HC - 1)),
                       waits=w, inc=(sPE if hc == HC - 1 else None))
            war_wring[i] = sPE.n
            vmm = sPE.n
            rec.op("scalar", lambda e, oc=oc, pb=pb, i=i:
                   e.activation(tmp[i][:, :], pb[:, 0:TW], AF.Identity, bias=col("bout", oc)),
                   waits=[(sPE, vmm), (sDVE, ev_tmp[i])], inc=sACT)
            war_po[i] = sACT.n
            vev = sACT.n
            rec.op("vector", lambda e, oc=oc, i=i: e.tensor_tensor(
                s_t2[:, oc * TW:(oc + 1) * TW], s_t2[:, oc * TW:(oc + 1) * TW],
                tmp[i][:, :], ALU.add),
                waits=[(sACT, vev), (sLY, vs)])
            rec.op("vector", lambda e, oc=oc: e.tensor_tensor(
                xo_t2[:, oc * TW:(oc + 1) * TW], xo_t2[:, oc * TW:(oc + 1) * TW],
                s_t2[:, oc * TW:(oc + 1) * TW], ALU.add),
                waits=[(sLY, vxo)] if oc == 0 else (), inc=sDVE)
            ev_tmp[i] = sDVE.n
        vdone = ln_tile(lambda c: xo_t2[:, c * TW:(c + 1) * TW], None,
                        [(lambda c: s_t2[:, c * TW:(c + 1) * TW], "f32"),
                         (lambda c: x3bf_t[:, c * TW:(c + 1) * TW], "bf16")],
                        xbf5, sqbf5, src_ready=[(sST, st_x3)])
        ev_tmp[0] = ev_tmp[1] = vdone
        rec.op("gpsimd", lambda e, tt=tt: e.dma_start(
            x3v[:, :, tt * TW:(tt + 1) * TW], s_t2[:, :].rearrange("p (c n) -> p c n", n=TW)),
            waits=[(sDVE, vdone)], inc=sST, inc_k=16)
        rec.op("gpsimd", lambda e, tt=tt: e.dma_start(
            x3bv[:, :, tt * TW:(tt + 1) * TW], x3bf_t[:, :].rearrange("p (c n) -> p c n", n=TW)),
            inc=sST, inc_k=16)
        st_x3 = sST.n
        for grp in range(4):
            i = grp % 2
            pb = P[6 + i]
            for c in range(C):
                w = [(sDVE, vdone), (sDVE, war_rps[i])] if c == 0 else []
                rec.op("tensor", lambda e, c=c, grp=grp, pb=pb:
                       e.matmul(pb[0:128, 0:NEX],
                                x3bf_t[:, c * TW + grp * 128: c * TW + grp * 128 + 128],
                                gateWT_bf[:, c * NEX:(c + 1) * NEX],
                                start=(c == 0), stop=(c == C - 1)),
                       waits=w, inc=(sPE if c == C - 1 else None))
            vmm = sPE.n
            vop(lambda e, pb=pb: e.tensor_copy(g_tm[:, :], pb[0:128, 0:NEX]),
                extra=[(sPE, vmm), (sACT, ev_router_act[0])])
            vop(lambda e: e.reduce_max(cc[:, 2:3], g_tm[:, :], axis=AX.X))
            vop(lambda e: e.tensor_single_scalar(g_t2[:, :], g_tm[:, :], cc[:, 2:3], ALU.subtract))
            vop(lambda e: e.tensor_single_scalar(g_t2[:, :], g_t2[:, :], 0.0, ALU.is_ge))
            vop(lambda e: e.scalar_tensor_tensor(g_t1[:, :], g_t2[:, :], -1e30, g_tm[:, :], ALU.mult, ALU.add))
            vop(lambda e: e.reduce_max(cc[:, 3:4], g_t1[:, :], axis=AX.X))
            vop(lambda e: e.tensor_scalar_mul(cc[:, 4:5], cc[:, 2:3], -1.0))
            vd = vop(lambda e: e.tensor_tensor(cc[:, 5:6], cc[:, 2:3], cc[:, 3:4], ALU.subtract))
            vop(lambda e: e.activation(g_t1[:, :], g_tm[:, :], AF.Exp, bias=cc[:, 4:5]),
                extra=[(sDVE, vd)], engine="scalar")
            va = vop(lambda e: e.activation(cc[:, 6:7], cc[:, 5:6], AF.Sigmoid),
                     engine="scalar")
            ev_router_act[0] = va
            vop(lambda e: e.tensor_single_scalar(g_t2[:, :], g_tm[:, :], cc[:, 3:4], ALU.is_ge),
                extra=[(sACT, va)])
            vop(lambda e: e.tensor_tensor(g_tm[:, :], g_t1[:, :], g_t2[:, :], ALU.mult))
            vg = vop(lambda e: e.tensor_scalar_mul(g_tm[:, :], g_tm[:, :], cc[:, 6:7]))
            rec.op("tensor", lambda e, pb=pb: e.transpose(pb[0:NEX, 0:128], g_tm[:, :], ident[:, :]),
                   waits=[(sDVE, vg)], inc=sPE)
            vtr = sPE.n
            rec.op("vector", lambda e, tt=tt, grp=grp, pb=pb: e.tensor_copy(
                g_fm[0:NEX, tt * TW + grp * 128: tt * TW + grp * 128 + 128], pb[0:NEX, 0:128]),
                waits=[(sPE, vtr)], inc=sDVE)
            war_rps[i] = sDVE.n
    quiesce("A4b")
    regA4b.close()
    regA4.close()
    regAW.close()

    regB1 = contextlib.ExitStack()
    sbB1 = lambda name, shape, dt: regB1.enter_context(nc.sbuf_tensor(name, shape, dt))
    W1sb = sbB1("W1sb", [128, K2 * C * 128], BF16)
    W2sb = sbB1("W2sb", [128, C * K2 * 128], BF16)
    sh_sb = sbB1("sh_sb", [128, K2 * TW], BF16)
    x3bf_b = sbB1("x3bf_b", [128, C * TW], BF16)
    acc_t = [sbB1(f"acc_t{i}", [128, TW], F32) for i in range(2)]

    accv = dview(acc_d)
    war_x3bf = 0
    war_sh = 0
    war_ph = [0, 0]
    war_po2 = [0, 0]
    war_ge = 0
    ev_acc = [0, 0]
    st_acc = [[0] * C for _ in range(NTT)]
    st_buf = [0, 0]

    def w1_quarter(E, qtr, wait_pe):
        rec.op("sync", lambda e, E=E, qtr=qtr: e.dma_start(
            W1sb[:, qtr * 8192:(qtr + 1) * 8192].rearrange("p (a f) -> p a f", a=8),
            exp1bf_d[E, qtr * 8: (qtr + 1) * 8].rearrange("a p f -> p a f")),
            waits=[(sPE, wait_pe), (sST, cast_w1_done[E])], inc=sW1, inc_k=16)

    def w2_quarter(E, qtr, wait_pe):
        rec.op("sync", lambda e, E=E, qtr=qtr: e.dma_start(
            W2sb[:, qtr * 8192:(qtr + 1) * 8192].rearrange("p (o f) -> p o f", o=2),
            exp2bf_d[E, qtr * 2: (qtr + 1) * 2].rearrange("o p f -> p o f")),
            waits=[(sPE, wait_pe), (sST, cast_w2_done[E])], inc=sW2, inc_k=16)

    for qtr in range(4):
        w1_quarter(0, qtr, 0)
    vW1 = sW1.n
    for qtr in range(4):
        w2_quarter(0, qtr, 0)
    vW2 = sW2.n
    for E in range(NE if NE_RUN is None else NE_RUN):
        vW1_next = vW1
        vW2_next = vW2
        for tt in range(NTT):
            rec.op("sync", lambda e, tt=tt: e.dma_start(
                x3bf_b[:, :].rearrange("p (c n) -> p c n", n=TW), x3bv[:, :, tt * TW:(tt + 1) * TW]),
                waits=[(sPE, war_x3bf), (sST, st_x3)], inc=sLX, inc_k=16)
            vx3 = sLX.n
            if E > 0:
                rec.op("tensor", lambda e, E=E, tt=tt: e.matmul(
                    P[4][:, 0:TW], esel_bf[0:NEX, (E - 1) * 128: E * 128],
                    g_fm[0:NEX, tt * TW:(tt + 1) * TW], start=True, stop=True),
                    waits=[(sDVE, war_ge)], inc=sPE)
                vge = sPE.n
                rec.op("vector", lambda e: e.tensor_copy(Ge_sb[:, :], P[4][:, 0:TW]),
                       waits=[(sPE, vge)], inc=sDVE)
                war_ge = sDVE.n
                vge_d = sDVE.n
            else:
                vge_d = 0
            if ge_tap is not None and E == 1:
                rec.op("gpsimd", lambda e, tt=tt: e.dma_start(
                    ge_tap[:, tt * TW:(tt + 1) * TW], Ge_sb[:, :]),
                    waits=[(sDVE, vge_d)], inc=sST, inc_k=16)
            for k2 in range(K2):
                i = k2 % 2
                pb = P[i]
                for c in range(C):
                    w = []
                    if c == 0:
                        w = [(sLX, vx3), (sW1, vW1), (sACT, war_ph[i]), (sPE, war_sh)]
                    rec.op("tensor", lambda e, c=c, k2=k2, pb=pb:
                           e.matmul(pb[:, 0:TW], W1sb[:, k2 * 1024 + c * 128: k2 * 1024 + c * 128 + 128],
                                    x3bf_b[:, c * TW:(c + 1) * TW],
                                    start=(c == 0), stop=(c == C - 1)),
                           waits=w, inc=(sPE if c == C - 1 else None))
                vmm = sPE.n
                rec.op("scalar", lambda e, k2=k2, pb=pb:
                       e.activation(sh_sb[:, k2 * TW:(k2 + 1) * TW], pb[:, 0:TW], AF.Silu),
                       waits=[(sPE, vmm)], inc=sACT)
                war_ph[i] = sACT.n
                if tt == NTT - 1 and E + 1 < NE and (k2 % 8) == 7:
                    w1_quarter(E + 1, k2 // 8, sPE.n)
            if tt == NTT - 1 and E + 1 < NE:
                vW1_next = sW1.n
            vsilu = sACT.n
            war_x3bf = sPE.n
            for oc in range(C):
                i = oc % 2
                pb = P[2 + i]
                if E > 0:
                    rec.op("sync", lambda e, tt=tt, oc=oc, i=i: e.dma_start(
                        acc_t[i][:, :].rearrange("p (a n) -> p a n", a=1),
                        accv[:, oc:oc + 1, tt * TW:(tt + 1) * TW]),
                        waits=[(sDVE, ev_acc[i]),
                               (sST, max(st_acc[tt][oc], st_buf[i]))],
                        inc=sLZ, inc_k=16)
                    vacc = sLZ.n
                for k2 in range(K2):
                    w = []
                    if k2 == 0:
                        w = [(sW2, vW2), (sACT, vsilu), (sDVE, war_po2[i])]
                    rec.op("tensor", lambda e, k2=k2, oc=oc, pb=pb:
                           e.matmul(pb[:, 0:TW], W2sb[:, oc * 4096 + k2 * 128: oc * 4096 + k2 * 128 + 128],
                                    sh_sb[:, k2 * TW:(k2 + 1) * TW],
                                    start=(k2 == 0), stop=(k2 == K2 - 1)),
                           waits=w, inc=(sPE if k2 == K2 - 1 else None))
                vmm = sPE.n
                if E == 0:
                    rec.op("vector", lambda e, i=i, pb=pb: e.tensor_copy(acc_t[i][:, :], pb[:, 0:TW]),
                           waits=[(sPE, vmm), (sST, st_buf[i])], inc=sDVE)
                else:
                    rec.op("vector", lambda e, i=i, pb=pb: e.tensor_tensor(
                        tmp[2 + i][:, :], pb[:, 0:TW], Ge_sb[:, :], ALU.mult),
                        waits=[(sPE, vmm), (sDVE, vge_d), (sLZ, vacc)])
                    rec.op("vector", lambda e, i=i: e.tensor_tensor(
                        acc_t[i][:, :], acc_t[i][:, :], tmp[2 + i][:, :], ALU.add),
                        inc=sDVE)
                war_po2[i] = sDVE.n
                ev_acc[i] = sDVE.n
                vadd = sDVE.n
                rec.op("gpsimd", lambda e, tt=tt, oc=oc, i=i: e.dma_start(
                    accv[:, oc:oc + 1, tt * TW:(tt + 1) * TW],
                    acc_t[i][:, :].rearrange("p (a n) -> p a n", a=1)),
                    waits=[(sDVE, vadd)], inc=sST, inc_k=16)
                st_acc[tt][oc] = sST.n
                st_buf[i] = sST.n
                if tt == NTT - 1 and E + 1 < NE and (oc % 2) == 1:
                    w2_quarter(E + 1, oc // 2, sPE.n)
            if tt == NTT - 1 and E + 1 < NE:
                vW2_next = sW2.n
            war_sh = sPE.n
        vW1 = vW1_next
        vW2 = vW2_next
    quiesce("B1")
    regB1.close()

    regB2 = contextlib.ExitStack()
    sbB2 = lambda name, shape, dt: regB2.enter_context(nc.sbuf_tensor(name, shape, dt))
    accB = [sbB2(f"accB{i}", [128, C * TW], F32) for i in range(2)]
    x3B = [sbB2(f"x3B{i}", [128, C * TW], F32) for i in range(2)]
    z2B = sbB2("z2B", [128, C * TW], F32)
    outB = sbB2("outB", [128, C * TW], F32)
    xbf6 = sbB2("xbf6", [128, C * TW], BF16)
    sqbf6 = sbB2("sqbf6", [128, C * TW], BF16)

    outv = dview(out_d)
    st_out = 0
    vldB = [0] * NTT
    zwar = [0, 0]

    def b2_load(tt):
        rec.op("sync", lambda e, tt=tt: e.dma_start(
            accB[tt % 2][:, :].rearrange("p (c n) -> p c n", n=TW), accv[:, :, tt * TW:(tt + 1) * TW]),
            waits=[(sDVE, zwar[tt % 2])], inc=sLX, inc_k=16)
        rec.op("sync", lambda e, tt=tt: e.dma_start(
            x3B[tt % 2][:, :].rearrange("p (c n) -> p c n", n=TW), x3v[:, :, tt * TW:(tt + 1) * TW]),
            inc=sLX, inc_k=16)
        vldB[tt] = sLX.n

    b2_load(0)
    b2_load(1)
    for tt in range(NTT):
        for c in range(C):
            w = [(sLX, vldB[tt])] if c == 0 else []
            rec.op("vector", lambda e, c=c, tt=tt: e.tensor_tensor(
                z2B[:, c * TW:(c + 1) * TW], x3B[tt % 2][:, c * TW:(c + 1) * TW],
                accB[tt % 2][:, c * TW:(c + 1) * TW], ALU.add),
                waits=w, inc=(sDVE if c == C - 1 else None))
        zwar[tt % 2] = sDVE.n
        if tt + 2 < NTT:
            b2_load(tt + 2)
        vz = sDVE.n
        vdone = ln_tile(lambda c: z2B[:, c * TW:(c + 1) * TW], None,
                        [(lambda c: outB[:, c * TW:(c + 1) * TW], "f32")],
                        xbf6, sqbf6, src_ready=[(sDVE, vz)])
        rec.op("gpsimd", lambda e, tt=tt: e.dma_start(
            outv[:, :, tt * TW:(tt + 1) * TW], outB[:, :].rearrange("p (c n) -> p c n", n=TW)),
            waits=[(sDVE, vdone), (sST, st_out)], inc=sST, inc_k=16)
        st_out = sST.n

    if gfm_tap is not None:
        rec.op("gpsimd", lambda e: e.dma_start(gfm_tap[:, :], g_fm[0:NEX, :]),
               waits=[(sST, sST.n)], inc=sST, inc_k=16)
    for (tap_d, src_d) in tap_outs:
        rec.op("gpsimd", lambda e, tap_d=tap_d, src_d=src_d: e.dma_start(
            tap_d[:, :], src_d[:, :]), waits=[(sST, sST.n)], inc=sST, inc_k=16)

    quiesce("B2")

    with nc.Block() as block:
        block.gpsimd(lambda e: e.wait_ge(sST.h, sST.n))

    return nc


def _pack_inputs(inputs):
    f = np.float32
    A = lambda v: np.ascontiguousarray(np.asarray(v, dtype=f))

    x = A(inputs["L0_attn_x"])
    Wq = A(inputs["L0_attn_wq"]); Wk = A(inputs["L0_attn_wk"])
    Wv = A(inputs["L0_attn_wv"]); Wg = A(inputs["L0_attn_wg"])
    Wo = A(inputs["L0_attn_wo"])
    w_in = A(inputs["L0_attn_w_in"])
    w_out = A(inputs["L0_attn_w_out"])
    sh1 = A(inputs["L0_ffn_shared_w1"])
    sh2 = A(inputs["L0_ffn_shared_w2"])
    ew1 = A(inputs["L0_ffn_experts_w1"])
    ew2 = A(inputs["L0_ffn_experts_w2"])
    gw = A(inputs["L0_ffn_gate_w"])

    wu = np.zeros((D, HIDP), f); wu[:, :HID] = w_in[:, :HID]
    wgg = np.zeros((D, HIDP), f); wgg[:, :HID] = w_in[:, HID:]
    winU = np.ascontiguousarray(wu.reshape(C, 128, HC, 128).transpose(2, 1, 0, 3))
    winG = np.ascontiguousarray(wgg.reshape(C, 128, HC, 128).transpose(2, 1, 0, 3))
    wop = np.zeros((HIDP, D), f); wop[:HID] = w_out
    woutP = np.ascontiguousarray(wop.reshape(HC, 128, C, 128).transpose(2, 1, 0, 3))

    e1 = np.empty((NE, K2, 128, C, 128), f)
    e2 = np.empty((NE, C, 128, K2, 128), f)
    for E in range(NE):
        w1 = sh1 if E == 0 else ew1[E - 1]
        w2 = sh2 if E == 0 else ew2[E - 1]
        w1T = np.ascontiguousarray(w1.T)
        w2T = np.ascontiguousarray(w2.T)
        e1[E] = w1T.reshape(C, 128, K2, 128).transpose(2, 1, 0, 3)
        e2[E] = w2T.reshape(K2, 128, C, 128).transpose(2, 1, 0, 3)

    gateWT = np.ascontiguousarray(gw.T)

    pc = np.zeros((128, PCOLS_W), f)
    def setc(name, vec, n):
        pc[:, PCOL_OFF[name]:PCOL_OFF[name] + n] = np.asarray(vec, f).reshape(n, 128).T
    setc("n1w", inputs["L0_attn_norm1_w"], 8); setc("n1b", inputs["L0_attn_norm1_b"], 8)
    setc("n2w", inputs["L0_attn_norm2_w"], 8); setc("n2b", inputs["L0_attn_norm2_b"], 8)
    setc("gnw", inputs["L0_attn_gn_w"], 8); setc("gnb", inputs["L0_attn_gn_b"], 8)
    setc("bq", inputs["L0_attn_bq"], 8); setc("bk", inputs["L0_attn_bk"], 8)
    setc("bg", inputs["L0_attn_bg"], 8); setc("bo", inputs["L0_attn_bo"], 8)
    setc("bout", inputs["L0_attn_b_out"], 8)
    b_in = A(inputs["L0_attn_b_in"])
    biu = np.zeros(HIDP, f); biu[:HID] = b_in[:HID]
    big = np.zeros(HIDP, f); big[:HID] = b_in[HID:]
    setc("binu", biu, HC); setc("bing", big, HC)
    gam = 1.0 - 2.0 ** (-5.0 - np.arange(H, dtype=np.float64))
    lng = np.log(gam)
    pp, ccx = np.meshgrid(np.arange(128), np.arange(8), indexing="ij")
    hmap = 2 * ccx + pp // 64
    pc[:, PCOL_OFF["lng"]:PCOL_OFF["lng"] + 8] = lng[hmap].astype(f)
    pc[:, PCOL_OFF["nlng"]:PCOL_OFF["nlng"] + 8] = (-lng[hmap]).astype(f)

    m, n = np.meshgrid(np.arange(128), np.arange(TW), indexing="ij")
    masks = np.concatenate([(n - m >= 128 * j).astype(f) for j in range(4)], axis=1)
    masks = np.ascontiguousarray(masks)
    R2 = np.broadcast_to(np.arange(TW, dtype=f), (128, TW)).copy()
    ident = np.eye(128, dtype=f)
    esel = np.zeros((NEX, NEX * 128), f)
    for e in range(NEX):
        esel[e, e * 128:(e + 1) * 128] = 1.0
    bv_row = A(inputs["L0_attn_bv"]).reshape(1, D)
    sel2 = np.zeros((2, 128), f)
    sel2[0, 0:64] = 1.0
    sel2[1, 64:128] = 1.0
    selT = np.ascontiguousarray(sel2.T)

    shared = dict(Wq=Wq, Wk=Wk, Wv=Wv, Wg=Wg, Wo=Wo, winU=winU, winG=winG,
                  woutP=woutP, exp1=e1, exp2=e2, gateWT=gateWT, pcols_in=pc,
                  masks_in=masks, R2_in=R2, ident_in=ident, esel_in=esel, bv_row=bv_row,
                  sel2_in=sel2, selT_in=selT)
    in_maps = []
    for b in range(8):
        m_ = dict(shared)
        m_["x_in"] = np.ascontiguousarray(x[b].T)
        in_maps.append(m_)
    return in_maps


def kernel(**inputs):
    in_maps = _pack_inputs(inputs)
    nc = build_program()
    res = run_bass_kernel_spmd(nc, in_maps, core_ids=list(range(8)))
    out = np.stack([np.asarray(r["out"]).T for r in res.results], axis=0)
    return np.ascontiguousarray(out.astype(np.float32))


# version 11
# speedup vs baseline: 1.1985x; 1.1985x over previous
import sys
sys.path.insert(0, '/opt/trn_rl_repo')
import contextlib
import numpy as np

import concourse.bass as bass
import concourse.bass_utils as _bass_utils
import concourse.mybir as mybir
from concourse.bass_utils import run_bass_kernel_spmd

if not getattr(_bass_utils, "_ldw_opt_patched", False):
    _orig_run_command = _bass_utils.run_command

    def _run_command_ldw(cmd, *a, **kw):
        if isinstance(cmd, list):
            cmd = ["--enable-ldw-opt=true" if c == "--enable-ldw-opt=false" else c
                   for c in cmd]
        return _orig_run_command(cmd, *a, **kw)

    _bass_utils.run_command = _run_command_ldw
    _bass_utils._ldw_opt_patched = True

F32 = mybir.dt.float32
BF16 = mybir.dt.bfloat16
AF = mybir.ActivationFunctionType
ALU = mybir.AluOpType
AX = mybir.AxisListType

D = 1024
S = 2048
C = 8
NTT = 4
TW = 512
H = 16
HD = 64
HID = 2730
HIDP = 2816
HC = 22
MH = 4096
K2 = 32
NE = 9
NEX = 8
EPS = 1e-5
LN8 = float(np.log(8.0))

_PCOLS = [("n1w", 8), ("n1b", 8), ("n2w", 8), ("n2b", 8), ("gnw", 8), ("gnb", 8),
          ("bq", 8), ("bk", 8), ("bg", 8), ("bo", 8), ("bout", 8),
          ("binu", HC), ("bing", HC), ("lng", 8), ("nlng", 8)]
PCOL_OFF = {}
_o = 0
for _n, _k in _PCOLS:
    PCOL_OFF[_n] = _o
    _o += _k
PCOLS_W = _o


class Sem:
    def __init__(self, handle):
        self.h = handle
        self.n = 0


class Rec:
    ENGINES = ("tensor", "vector", "scalar", "gpsimd", "sync")

    def __init__(self, nc):
        self.nc = nc
        self.ev = {e: [] for e in self.ENGINES}

    def op(self, eng, emitfn, waits=(), inc=None, inc_k=1):
        if inc is not None:
            inc.n += inc_k
        waits = tuple((s.h, v) for (s, v) in waits if v > 0)
        ih = None if inc is None else inc.h

        def f(e, emitfn=emitfn, waits=waits, ih=ih, inc_k=inc_k):
            for (h, v) in waits:
                e.wait_ge(h, v)
            ins = emitfn(e)
            if ih is not None:
                ins.then_inc(ih, inc_k)
        self.ev[eng].append(f)

    def wait(self, eng, sem, v):
        if v > 0:
            self.ev[eng].append(lambda e, h=sem.h, v=v: e.wait_ge(h, v))

    def flush(self, scope=None):
        ev = self.ev
        self.ev = {e: [] for e in self.ENGINES}
        import contextlib as _cl
        cm = self.nc.named_scope(scope) if scope else _cl.nullcontext()
        with cm:
            with self.nc.Block() as block:
                for name, reg in (("tensor", block.tensor), ("vector", block.vector),
                                  ("scalar", block.scalar), ("gpsimd", block.gpsimd),
                                  ("sync", block.sync)):
                    if ev[name]:
                        reg(lambda e, fs=tuple(ev[name]): [f(e) for f in fs])


NE_RUN = None

def build_program(debug_taps=()):
    nc = bass.Bass("TRN2", target_bir_lowering=False)
    ctx = contextlib.ExitStack()

    di = lambda name, shape: nc.dram_tensor(name, shape, F32, kind="ExternalInput")
    x_in = di("x_in", [D, S])
    Wq = di("Wq", [D, D]); Wk = di("Wk", [D, D]); Wv = di("Wv", [D, D])
    Wg = di("Wg", [D, D]); Wo = di("Wo", [D, D])
    winU = di("winU", [HC, 128, C, 128])
    winG = di("winG", [HC, 128, C, 128])
    woutP = di("woutP", [C, 128, HC, 128])
    exp1 = di("exp1", [NE, K2, 128, C, 128])
    exp2 = di("exp2", [NE, C, 128, K2, 128])
    gateWT_in = di("gateWT", [D, NEX])
    pcols_in = di("pcols_in", [128, PCOLS_W])
    masks_in = di("masks_in", [128, 4 * TW])
    R2_in = di("R2_in", [128, TW])
    ident_in = di("ident_in", [128, 128])
    esel_in = di("esel_in", [NEX, NEX * 128])
    sel2_in = di("sel2_in", [2, 128])
    selT_in = di("selT_in", [128, 2])
    bv_in = di("bv_row", [1, D])

    out_d = nc.dram_tensor("out", [D, S], F32, kind="ExternalOutput")

    dsc = lambda name, dt: nc.dram_tensor(name, [D, S], dt, kind="Internal")
    q_d = dsc("q_d", BF16)
    gate_d = dsc("gate_d", BF16)
    s_d = dsc("s_d", F32)
    woutbf_d = nc.dram_tensor("woutbf_d", [C, 128, HC * 128], BF16, kind="Internal")
    exp1bf_d = nc.dram_tensor("exp1bf_d", [NE, K2, 128, C * 128], BF16, kind="Internal")
    exp2bf_d = nc.dram_tensor("exp2bf_d", [NE, C, 128, K2 * 128], BF16, kind="Internal")
    x3_d = dsc("x3_d", F32)
    x3bf_d = dsc("x3bf_d", BF16)
    acc_d = dsc("acc_d", F32)

    tap_outs = []
    gfm_tap = None
    ge_tap = None
    for t in debug_taps:
        if t == "ge":
            ge_tap = nc.dram_tensor("tap_ge", [128, S], F32, kind="ExternalOutput")
            continue
        if t == "gfm":
            gfm_tap = nc.dram_tensor("tap_gfm", [NEX, S], BF16, kind="ExternalOutput")
            continue
        src = {"xn": xn_d, "q": q_d, "gate": gate_d, "s": s_d, "x3": x3_d,
               "x3bf": x3bf_d, "acc": acc_d}[t]
        dt = BF16 if t in ("xn", "q", "gate", "x3bf") else F32
        tap_outs.append((nc.dram_tensor("tap_" + t, [D, S], dt, kind="ExternalOutput"), src))

    def dview(d):
        return d[:, :].rearrange("(c p) t -> p c t", p=128)

    sems = {}
    def mksem(name):
        sems[name] = Sem(ctx.enter_context(nc.semaphore(name)))
        return sems[name]
    sPE = mksem("sPE"); sACT = mksem("sACT"); sDVE = mksem("sDVE")
    sLX = mksem("sLX")
    sLY = mksem("sLY")
    sLZ = mksem("sLZ")
    sWA = mksem("sWA")
    sW1 = mksem("sW1"); sW2 = mksem("sW2")
    sST = mksem("sST")
    ALL_DMA = (sLX, sLY, sLZ, sWA, sW1, sW2, sST)

    sb_root = lambda name, shape, dt: ctx.enter_context(nc.sbuf_tensor(name, shape, dt))
    pcols = sb_root("pcols", [128, PCOLS_W], F32)
    esel_bf = sb_root("esel_bf", [NEX, NEX * 128], BF16)
    g_ones = sb_root("g_ones", [1, S], BF16)
    g_fm = sb_root("g_fm", [NEX, S], BF16)
    g_tm = sb_root("g_tm", [128, NEX], F32)
    g_t1 = sb_root("g_t1", [128, NEX], F32)
    g_t2 = sb_root("g_t2", [128, NEX], F32)
    ones_col = sb_root("ones_col", [128, 1], BF16)
    ones_row = sb_root("ones_row", [1, 128], BF16)
    eps_col = sb_root("eps_col", [128, 1], F32)
    cc = sb_root("cc", [128, 8], F32)
    r_mu = sb_root("r_mu", [2, TW], F32)
    r_var = sb_root("r_var", [2, TW], F32)
    r_r = sb_root("r_r", [2, TW], F32)
    r_u = sb_root("r_u", [2, TW], F32)
    r_rbf = sb_root("r_rbf", [2, TW], BF16)
    r_ubf = sb_root("r_ubf", [2, TW], BF16)
    tmp = [sb_root(f"tmp{i}", [128, TW], F32) for i in range(4)]
    Ge_sb = sb_root("Ge_sb", [128, TW], F32)

    P = [ctx.enter_context(nc.psum_tensor(f"p{i}", [128, 512], F32)) for i in range(8)]

    rec = Rec(nc)
    def col(name, i=0, r0=0, r1=128):
        off = PCOL_OFF[name] + i
        return pcols[r0:r1, off:off + 1]

    def quiesce(scope=None):
        for eng, ss in (("gpsimd", (sWA, sW1, sW2, sST)), ("sync", (sLX, sLY, sLZ))):
            for s in ss:
                rec.wait(eng, s, s.n)
        rec.flush(scope)

    def ph0(g):
        for s in sems.values():
            g.sem_clear(s.h)
        g.memset(ones_col[:, :], 1.0)
        g.memset(ones_row[:, :], 1.0)
        g.memset(g_ones[:, :], 1.0)
        g.memset(eps_col[:, :], EPS)
    rec.ev["gpsimd"].append(ph0)
    rec.flush()
    rec.op("gpsimd", lambda e: e.dma_start(esel_bf[:, :], esel_in[:, :]),
           inc=sWA, inc_k=16)
    rec.op("sync", lambda e: e.dma_start(pcols[:, :], pcols_in[:, :]),
           inc=sLX, inc_k=16)
    quiesce("init1")

    regAW = ctx.enter_context(contextlib.ExitStack())
    sbAW = lambda name, shape, dt: regAW.enter_context(nc.sbuf_tensor(name, shape, dt))
    masks_bf = sbAW("masks_bf", [128, 4 * TW], BF16)
    R2 = sbAW("R2", [128, TW], F32)
    qsc = [sbAW(f"qsc{i}", [128, TW], F32) for i in range(2)]
    sel2_bf = sbAW("sel2_bf", [2, 128], BF16)
    selT_bf = sbAW("selT_bf", [128, 2], BF16)
    bv_bf = sbAW("bv_bf", [1, D], BF16)
    gateWT_bf = sbAW("gateWT_bf", [128, C * NEX], BF16)
    ident = sbAW("ident", [128, 128], F32)

    rec.op("gpsimd", lambda e: e.dma_start(masks_bf[:, :], masks_in[:, :]),
           inc=sWA, inc_k=16)
    rec.op("gpsimd", lambda e: e.dma_start(bv_bf[:, :], bv_in[:, :]),
           inc=sWA, inc_k=16)
    rec.op("gpsimd", lambda e: e.dma_start(
        gateWT_bf[:, :].rearrange("p (c e) -> p c e", e=NEX),
        gateWT_in[:, :].rearrange("(c p) e -> p c e", p=128)),
        inc=sWA, inc_k=16)
    rec.op("gpsimd", lambda e: e.dma_start(sel2_bf[:, :], sel2_in[:, :]), inc=sWA, inc_k=16)
    rec.op("gpsimd", lambda e: e.dma_start(selT_bf[:, :], selT_in[:, :]), inc=sWA, inc_k=16)
    rec.op("sync", lambda e: e.dma_start(R2[:, :], R2_in[:, :]), inc=sLX, inc_k=16)
    rec.op("sync", lambda e: e.dma_start(ident[:, :], ident_in[:, :]), inc=sLX, inc_k=16)
    quiesce("init2")

    def vop(emit, extra=(), engine="vector", sem=None):
        sem = sem or (sDVE if engine == "vector" else sACT)
        prev = sem.n
        rec.op(engine, emit, waits=list(extra) + [(sem, prev)], inc=sem)
        return sem.n

    def ln_rows(inv_n, pA, pB, vstat, nr=1):
        vop(lambda e: e.tensor_scalar_mul(r_mu[0:nr, :], pA[0:nr, 0:TW], inv_n),
            extra=[(sPE, vstat)])
        vop(lambda e: e.tensor_scalar_mul(r_var[0:nr, :], pB[0:nr, 0:TW], inv_n))
        vop(lambda e: e.tensor_tensor(r_u[0:nr, :], r_mu[0:nr, :], r_mu[0:nr, :], ALU.mult))
        vv = vop(lambda e: e.tensor_tensor(r_var[0:nr, :], r_var[0:nr, :], r_u[0:nr, :], ALU.subtract))
        vop(lambda e: e.activation(r_var[0:nr, :], r_var[0:nr, :], AF.Ln,
                                   bias=eps_col[0:nr, 0:1]),
            extra=[(sDVE, vv)], engine="scalar")
        va = vop(lambda e: e.activation(r_r[0:nr, :], r_var[0:nr, :], AF.Exp, scale=-0.5),
                 engine="scalar")
        vop(lambda e: e.tensor_tensor(r_u[0:nr, :], r_mu[0:nr, :], r_r[0:nr, :], ALU.mult),
            extra=[(sACT, va)])
        vop(lambda e: e.tensor_copy(r_rbf[0:nr, :], r_r[0:nr, :]))
        vop(lambda e: e.tensor_copy(r_ubf[0:nr, :], r_u[0:nr, :]))
        return sDVE.n

    def ln_tile(src_fn, aff, writers, x_bf, sq_bf, src_ready=(), n_feat=D):
        for c in range(C):
            w = list(src_ready) if c == 0 else []
            rec.op("vector", lambda e, c=c: e.tensor_copy(x_bf[:, c * TW:(c + 1) * TW], src_fn(c)),
                   waits=w, inc=(sDVE if c == C - 1 else None))
        vcast = sDVE.n
        for c in range(C):
            rec.op("scalar", lambda e, c=c: e.activation(sq_bf[:, c * TW:(c + 1) * TW],
                                                         x_bf[:, c * TW:(c + 1) * TW], AF.Square),
                   waits=[(sDVE, vcast)] if c == 0 else (), inc=(sACT if c == C - 1 else None))
        vsq = sACT.n
        for c in range(C):
            rec.op("tensor", lambda e, c=c: e.matmul(P[2][0:1, 0:TW], ones_col[:, 0:1],
                                                     x_bf[:, c * TW:(c + 1) * TW],
                                                     start=(c == 0), stop=(c == C - 1)),
                   waits=[(sDVE, vcast)] if c == 0 else ())
        for c in range(C):
            rec.op("tensor", lambda e, c=c: e.matmul(P[3][0:1, 0:TW], ones_col[:, 0:1],
                                                     sq_bf[:, c * TW:(c + 1) * TW],
                                                     start=(c == 0), stop=(c == C - 1)),
                   waits=[(sACT, vsq)] if c == 0 else (),
                   inc=(sPE if c == C - 1 else None))
        vrows = ln_rows(1.0 / n_feat, P[2], P[3], sPE.n)
        rec.op("tensor", lambda e: e.matmul(P[4][:, 0:TW], ones_row[0:1, :], r_rbf[0:1, :],
                                            start=True, stop=True),
               waits=[(sDVE, vrows)])
        rec.op("tensor", lambda e: e.matmul(P[5][:, 0:TW], ones_row[0:1, :], r_ubf[0:1, :],
                                            start=True, stop=True), inc=sPE)
        vbc = sPE.n
        for c in range(C):
            tm = tmp[c % 2]
            rec.op("vector", lambda e, c=c, tm=tm: e.tensor_tensor(tm[:, :], src_fn(c),
                                                                   P[4][:, 0:TW], ALU.mult),
                   waits=[(sPE, vbc)] if c == 0 else ())
            rec.op("vector", lambda e, tm=tm: e.tensor_tensor(tm[:, :], tm[:, :],
                                                              P[5][:, 0:TW], ALU.subtract))
            for wi, (dst_fn, mode) in enumerate(writers):
                last = (c == C - 1 and wi == len(writers) - 1)
                if mode == "aff":
                    wname, bname = aff
                    rec.op("vector", lambda e, c=c, tm=tm, dst_fn=dst_fn, wname=wname, bname=bname:
                           e.tensor_scalar(dst_fn(c), tm[:, :], col(wname, c), col(bname, c),
                                           ALU.mult, ALU.add),
                           inc=(sDVE if last else None))
                else:
                    rec.op("vector", lambda e, c=c, tm=tm, dst_fn=dst_fn:
                           e.tensor_copy(dst_fn(c), tm[:, :]),
                           inc=(sDVE if last else None))
        return sDVE.n

    ev_tmp = [0, 0]
    ev_router_act = [0]

    cast_w1_done = [0] * NE
    cast_w2_done = [0] * NE
    _cast_q = []
    for E in range(2, NE):
        for half in range(2):
            _cast_q.append(("w1", E, half))
        for half in range(2):
            _cast_q.append(("w2", E, half))

    def drip_cast(n=1):
        for _ in range(n):
            if not _cast_q:
                return
            kind, E, half = _cast_q.pop(0)
            if kind == "w1":
                rec.op("gpsimd", lambda e, E=E, half=half: e.dma_start(
                    exp1bf_d[E, half * 16:(half + 1) * 16],
                    exp1[E, half * 16:(half + 1) * 16].rearrange("a p k m -> a p (k m)")),
                    inc=sST, inc_k=16)
                if half == 1:
                    cast_w1_done[E] = sST.n
            else:
                rec.op("gpsimd", lambda e, E=E, half=half: e.dma_start(
                    exp2bf_d[E, half * 4:(half + 1) * 4],
                    exp2[E, half * 4:(half + 1) * 4].rearrange("a p k m -> a p (k m)")),
                    inc=sST, inc_k=16)
                if half == 1:
                    cast_w2_done[E] = sST.n

    regSH = ctx.enter_context(contextlib.ExitStack())
    sbSH = lambda name, shape, dt: regSH.enter_context(nc.sbuf_tensor(name, shape, dt))
    k_sb = sbSH("k_sb", [128, C * S], BF16)
    v_sb = sbSH("v_sb", [128, 16 * D], BF16)
    gate_t = sbSH("gate_t", [128, C * TW], BF16)
    wo_sb = sbSH("wo_sb", [128, C * D], BF16)
    q_t = sbSH("q_t", [128, C * TW], BF16)

    regA12 = contextlib.ExitStack()
    sbA12 = lambda name, shape, dt: regA12.enter_context(nc.sbuf_tensor(name, shape, dt))
    xn_sb = sbA12("xn_sb", [128, C * S], BF16)

    x_war = [0, 0]
    regA12a = contextlib.ExitStack()
    sbA12a = lambda name, shape, dt: regA12a.enter_context(nc.sbuf_tensor(name, shape, dt))
    x_t = [sbA12a("x_t0", [128, C * TW], F32)]
    xbf_t = sbA12a("xbf_t", [128, C * TW], BF16)
    sqbf_t = sbA12a("sqbf_t", [128, C * TW], BF16)

    xv = dview(x_in)
    for tt in range(NTT):
        xt = x_t[0]
        rec.op("sync", lambda e, tt=tt, xt=xt: e.dma_start(
            xt[:, :].rearrange("p (c n) -> p c n", n=TW), xv[:, :, tt * TW:(tt + 1) * TW]),
            waits=[(sDVE, x_war[0])], inc=sLX, inc_k=16)
        vload = sLX.n
        vdone = ln_tile(lambda c, xt=xt: xt[:, c * TW:(c + 1) * TW], ("n1w", "n1b"),
                        [(lambda c, tt=tt: xn_sb[:, c * S + tt * TW: c * S + (tt + 1) * TW], "aff")],
                        xbf_t, sqbf_t, src_ready=[(sLX, vload)])
        ev_tmp[0] = ev_tmp[1] = vdone
        x_war[0] = vdone
    v_ln1 = sDVE.n
    quiesce("A12a")
    regA12a.close()

    regA12b = contextlib.ExitStack()
    sbA12b = lambda name, shape, dt: regA12b.enter_context(nc.sbuf_tensor(name, shape, dt))
    wbuf = [sbA12b(f"wbuf{i}", [128, C * D], BF16) for i in range(2)]

    qv = dview(q_d)
    gv = dview(gate_d)
    st_gate = [0] * NTT

    def proj_weight_load(W, buf, after_pe):
        rec.op("gpsimd", lambda e, W=W, buf=buf: e.dma_start(
            buf[:, :].rearrange("p (k m) -> p k m", m=D),
            W[:, :].rearrange("(k p) m -> p k m", p=128)),
            waits=[(sPE, after_pe)], inc=sWA, inc_k=16)
        return sWA.n

    wbuf_war = [0, 0]
    ev_qsc = [0, 0]
    ev_cc = [0, 0]
    psum_war = [0, 0]

    projs = (("q", Wq), ("k", Wk), ("v", Wv), ("g", Wg))
    vw_all = []
    for pi, (proj, W) in enumerate(projs[:2]):
        vw_all.append(proj_weight_load(W, wbuf[pi % 2], wbuf_war[pi % 2]))
    for pi, (proj, W) in enumerate(projs):
        vw = vw_all[pi]
        wb = wbuf[pi % 2]
        for tt in range(NTT):
            vxn = v_ln1
            xn_l = xn_sb[:, :]
            if proj in ("q", "k"):
                for oc in range(C):
                    i = oc % 2
                    pb = P[6 + i]
                    lncol = "lng" if proj == "q" else "nlng"
                    bias2 = -LN8 if proj == "q" else 0.0
                    rec.op("vector", lambda e, oc=oc, i=i, lncol=lncol, bias2=bias2, tt=tt:
                           e.tensor_scalar(cc[:, i:i + 1], col(lncol, oc), float(tt * TW), bias2,
                                           ALU.mult, ALU.add),
                           waits=[(sACT, ev_cc[i])], inc=sDVE)
                    vb = sDVE.n
                    rec.op("scalar", lambda e, oc=oc, i=i, lncol=lncol:
                           e.activation(qsc[i][:, :], R2[:, :], AF.Exp,
                                        bias=cc[:, i:i + 1], scale=col(lncol, oc)),
                           waits=[(sDVE, max(vb, ev_qsc[i]))], inc=sACT)
                    ev_cc[i] = sACT.n
                    vgen = sACT.n
                    for c in range(C):
                        w = [(sDVE, vxn), (sWA, vw), (sACT, psum_war[i])] if c == 0 else []
                        rec.op("tensor", lambda e, c=c, oc=oc, tt=tt, pb=pb, wb=wb:
                               e.matmul(pb[:, 0:TW], wb[:, c * D + oc * 128: c * D + oc * 128 + 128],
                                        xn_sb[:, c * S + tt * TW: c * S + (tt + 1) * TW],
                                        start=(c == 0), stop=(c == C - 1)),
                               waits=w, inc=(sPE if c == C - 1 else None))
                    vmm = sPE.n
                    bcol = "bq" if proj == "q" else "bk"
                    rec.op("scalar", lambda e, pb=pb, i=i, oc=oc, bcol=bcol:
                           e.activation(tmp[i][:, :], pb[:, 0:TW], AF.Identity, bias=col(bcol, oc)),
                           waits=[(sPE, vmm), (sDVE, ev_tmp[i])], inc=sACT)
                    psum_war[i] = sACT.n
                    vev = sACT.n
                    dst_sb = q_t if proj == "q" else k_sb
                    if proj == "q":
                        dsl = lambda e, oc=oc: q_t[:, oc * TW:(oc + 1) * TW]
                    else:
                        dsl = lambda e, oc=oc, tt=tt: k_sb[:, oc * S + tt * TW: oc * S + (tt + 1) * TW]
                    rec.op("vector", lambda e, i=i, dsl=dsl:
                           e.tensor_tensor(dsl(e), tmp[i][:, :], qsc[i][:, :], ALU.mult),
                           waits=[(sACT, vgen), (sACT, vev)], inc=sDVE)
                    ev_tmp[i] = sDVE.n
                    ev_qsc[i] = sDVE.n
                if proj == "q":
                    rec.op("gpsimd", lambda e, tt=tt: e.dma_start(
                        qv[:, :, tt * TW:(tt + 1) * TW],
                        q_t[:, :].rearrange("p (c n) -> p c n", n=TW)),
                        waits=[(sDVE, sDVE.n)], inc=sST, inc_k=16)
                    rec.wait("vector", sST, sST.n)
            elif proj == "v":
                for grp in range(4):
                    gg_ = tt * 4 + grp
                    for half in range(2):
                        i = (grp * 2 + half) % 2
                        pb = P[6 + i]
                        for c in range(C):
                            w = [(sDVE, vxn), (sWA, vw), (sACT, psum_war[i])] if c == 0 else []
                            rec.op("tensor", lambda e, c=c, grp=grp, half=half, tt=tt, pb=pb, wb=wb:
                                   e.matmul(pb[:, 0:TW],
                                            xn_sb[:, c * S + tt * TW + grp * 128: c * S + tt * TW + grp * 128 + 128],
                                            wb[:, c * D + half * 512: c * D + (half + 1) * 512],
                                            start=(c == 0), stop=False),
                                   waits=w)
                        rec.op("tensor", lambda e, half=half, pb=pb:
                               e.matmul(pb[:, 0:TW], ones_row[0:1, :],
                                        bv_bf[0:1, half * 512:(half + 1) * 512],
                                        start=False, stop=True), inc=sPE)
                        vmm = sPE.n
                        rec.op("scalar", lambda e, gg_=gg_, half=half, pb=pb:
                               e.activation(v_sb[:, gg_ * D + half * 512: gg_ * D + (half + 1) * 512],
                                            pb[:, 0:TW], AF.Copy),
                               waits=[(sPE, vmm)], inc=sACT)
                        psum_war[i] = sACT.n
            else:
                for oc in range(C):
                    i = oc % 2
                    pb = P[6 + i]
                    for c in range(C):
                        w = [(sDVE, vxn), (sWA, vw), (sACT, psum_war[i])] if c == 0 else []
                        if c == 0 and tt > 0:
                            w.append((sST, st_gate[tt - 1]))
                        rec.op("tensor", lambda e, c=c, oc=oc, tt=tt, pb=pb, wb=wb:
                               e.matmul(pb[:, 0:TW], wb[:, c * D + oc * 128: c * D + oc * 128 + 128],
                                        xn_sb[:, c * S + tt * TW: c * S + (tt + 1) * TW],
                                        start=(c == 0), stop=(c == C - 1)),
                               waits=w, inc=(sPE if c == C - 1 else None))
                    vmm = sPE.n
                    rec.op("scalar", lambda e, oc=oc, pb=pb:
                           e.activation(gate_t[:, oc * TW:(oc + 1) * TW], pb[:, 0:TW],
                                        AF.Silu, bias=col("bg", oc)),
                           waits=[(sPE, vmm)], inc=sACT)
                    psum_war[i] = sACT.n
                rec.op("gpsimd", lambda e, tt=tt: e.dma_start(
                    gv[:, :, tt * TW:(tt + 1) * TW],
                    gate_t[:, :].rearrange("p (c n) -> p c n", n=TW)),
                    waits=[(sACT, sACT.n)], inc=sST, inc_k=16)
                st_gate[tt] = sST.n
        wbuf_war[pi % 2] = sPE.n
        if pi + 2 < len(projs):
            vw_all.append(proj_weight_load(projs[pi + 2][1], wbuf[pi % 2], wbuf_war[pi % 2]))

    rec.op("gpsimd", lambda e: e.dma_start(
        wo_sb[:, :].rearrange("p (k m) -> p k m", m=D),
        Wo[:, :].rearrange("(k p) m -> p k m", p=128)),
        inc=sWA, inc_k=16)
    vwo = sWA.n
    quiesce("A12")
    regA12b.close()
    regA12.close()

    regA3 = contextlib.ExitStack()
    sbA3 = lambda name, shape, dt: regA3.enter_context(nc.sbuf_tensor(name, shape, dt))
    s_t = sbA3("s_t", [128, C * TW], F32)
    xorig_t = sbA3("xorig_t", [128, C * TW], F32)
    gg_sb = sbA3("gg_sb", [128, C * TW], BF16)
    sTa = sbA3("sTa", [128, 4 * TW], BF16)
    sTd = sbA3("sTd", [128, 4 * TW], BF16)
    ret_bf = sbA3("ret_bf", [128, TW], BF16)
    retsq_bf = sbA3("retsq_bf", [128, TW], BF16)

    sv = dview(s_d)
    warB = {0: (sACT, 0), 1: (sACT, 0), 3: (sACT, 0), 4: (sACT, 0)}
    war_slotA = [0, 0, 0, 0]
    war_slotD = [0, 0, 0, 0]
    ev_gg = 0
    st_s_val = 0
    v_apply_prev = 0
    for qt in range(NTT):
        rec.op("sync", lambda e, qt=qt: e.dma_start(
            gate_t[:, :].rearrange("p (c n) -> p c n", n=TW), gv[:, :, qt * TW:(qt + 1) * TW]),
            waits=[(sDVE, ev_gg), (sST, st_gate[NTT - 1])], inc=sLY, inc_k=16)
        vgate = sLY.n
        rec.op("sync", lambda e, qt=qt: e.dma_start(
            xorig_t[:, :].rearrange("p (c n) -> p c n", n=TW), xv[:, :, qt * TW:(qt + 1) * TW]),
            waits=[(sDVE, sDVE.n)], inc=sLY, inc_k=16)
        vxorig = sLY.n
        rec.op("sync", lambda e, qt=qt: e.dma_start(
            q_t[:, :].rearrange("p (c n) -> p c n", n=TW), qv[:, :, qt * TW:(qt + 1) * TW]),
            waits=[(sPE, sPE.n)], inc=sLY, inc_k=16)
        vq = sLY.n

        nkt = 4 * qt + 4
        for hp in range(H // 2):
            c = hp
            banks = [[0, 1], [3, 4]]
            na = [0, 0]
            nd = [0, 0]
            ev_info = [dict(), dict()]
            def emit_sc(hi, kt):
                rb = hi * 64
                h = 2 * hp + hi
                bid = banks[hi][kt % 2]
                pb = P[bid]
                ws, wv = warB[bid]
                w = [(ws, wv)]
                if kt == 0:
                    w.append((sLY, vq))
                rec.op("tensor", lambda e, c=c, rb=rb, kt=kt, pb=pb:
                       e.matmul(pb[:, 0:TW],
                                k_sb[rb:rb + 64, c * S + kt * 128: c * S + kt * 128 + 128],
                                q_t[rb:rb + 64, c * TW:(c + 1) * TW],
                                start=True, stop=True),
                       waits=w, inc=sPE)
                return sPE.n, bid

            def emit_evac(hi, kt, vsc, bid):
                pb = P[bid]
                masked = (kt >= 4 * qt)
                if not masked:
                    sl = hi * 2 + (na[hi] % 2)
                    if hi == 0:
                        rec.op("scalar", lambda e, pb=pb, sl=sl:
                               e.activation(sTa[:, sl * TW:(sl + 1) * TW], pb[:, 0:TW], AF.Copy),
                               waits=[(sPE, vsc), (sPE, war_slotA[sl])], inc=sACT)
                        warB[bid] = (sACT, sACT.n)
                        ev_info[hi][kt] = (sACT, sACT.n, "a", sl)
                    else:
                        rec.op("vector", lambda e, pb=pb, sl=sl:
                               e.tensor_copy(sTa[:, sl * TW:(sl + 1) * TW], pb[:, 0:TW]),
                               waits=[(sPE, vsc), (sPE, war_slotA[sl])], inc=sDVE)
                        warB[bid] = (sDVE, sDVE.n)
                        ev_info[hi][kt] = (sDVE, sDVE.n, "a", sl)
                    na[hi] += 1
                else:
                    j = kt - 4 * qt
                    sl = hi * 2 + (nd[hi] % 2)
                    rec.op("vector", lambda e, pb=pb, sl=sl, j=j:
                           e.tensor_tensor(sTd[:, sl * TW:(sl + 1) * TW], pb[:, 0:TW],
                                           masks_bf[:, j * TW:(j + 1) * TW], ALU.mult),
                           waits=[(sPE, vsc), (sPE, war_slotD[sl])], inc=sDVE)
                    warB[bid] = (sDVE, sDVE.n)
                    ev_info[hi][kt] = (sDVE, sDVE.n, "d", sl)
                    nd[hi] += 1

            def emit_ret(hi, kt):
                rb = hi * 64
                h = 2 * hp + hi
                evsem, vev, kind, sl = ev_info[hi][kt]
                buf = sTa if kind == "a" else sTd
                w = [(evsem, vev)]
                if kt == 0:
                    w.append((sDVE, v_apply_prev))
                rec.op("tensor", lambda e, kt=kt, h=h, rb=rb, buf=buf, sl=sl, nkt=nkt:
                       e.matmul(P[2][rb:rb + 64, 0:TW],
                                v_sb[:, kt * D + h * 64: kt * D + h * 64 + 64],
                                buf[:, sl * TW:(sl + 1) * TW],
                                start=(kt == 0), stop=(kt == nkt - 1)),
                       waits=w, inc=sPE)
                if kind == "a":
                    war_slotA[sl] = sPE.n
                else:
                    war_slotD[sl] = sPE.n
                return sPE.n

            vsc_bid = [None, None]
            for kt in range(nkt):
                for hi in range(2):
                    vsc_bid[hi] = emit_sc(hi, kt)
                for hi in range(2):
                    emit_evac(hi, kt, *vsc_bid[hi])
                if kt > 0:
                    for hi in range(2):
                        emit_ret(hi, kt - 1)
            for hi in range(2):
                vret_last = emit_ret(hi, nkt - 1)
            rec.op("scalar", lambda e: e.activation(ret_bf[:, :], P[2][:, 0:TW], AF.Copy),
                   waits=[(sPE, vret_last), (sDVE, v_apply_prev)])
            rec.op("scalar", lambda e: e.activation(retsq_bf[:, :], ret_bf[:, :], AF.Square),
                   inc=sACT)
            vrsq = sACT.n
            rec.op("tensor", lambda e: e.matmul(P[5][0:2, 0:TW], selT_bf[:, 0:2],
                                                ret_bf[:, :], start=True, stop=True),
                   waits=[(sACT, vrsq)])
            rec.op("tensor", lambda e: e.matmul(P[6][0:2, 0:TW], selT_bf[:, 0:2],
                                                retsq_bf[:, :], start=True, stop=True),
                   inc=sPE)
            vrows = ln_rows(1.0 / 64.0, P[5], P[6], sPE.n, nr=2)
            rec.op("tensor", lambda e: e.matmul(P[7][:, 0:TW], sel2_bf[0:2, :],
                                                r_rbf[0:2, :], start=True, stop=True),
                   waits=[(sDVE, vrows)])
            rec.op("tensor", lambda e: e.matmul(P[2][:, 0:TW], sel2_bf[0:2, :],
                                                r_ubf[0:2, :], start=True, stop=True),
                   inc=sPE)
            vbc = sPE.n
            tm = tmp[2]
            rec.op("vector", lambda e, tm=tm: e.tensor_tensor(
                tm[:, :], ret_bf[:, :], P[7][:, 0:TW], ALU.mult),
                waits=[(sPE, vbc)])
            rec.op("vector", lambda e, tm=tm: e.tensor_tensor(
                tm[:, :], tm[:, :], P[2][:, 0:TW], ALU.subtract))
            rec.op("vector", lambda e, tm=tm, c=c: e.tensor_scalar(
                tm[:, :], tm[:, :], col("gnw", c), col("gnb", c), ALU.mult, ALU.add))
            rec.op("vector", lambda e, tm=tm, c=c: e.tensor_tensor(
                gg_sb[:, c * TW:(c + 1) * TW], tm[:, :],
                gate_t[:, c * TW:(c + 1) * TW], ALU.mult),
                waits=[(sLY, vgate)], inc=sDVE)
            v_apply_prev = sDVE.n
        ev_gg = sDVE.n
        for oc in range(C):
            i = oc % 2
            pb = P[i]
            ws, wv = warB[i]
            for c in range(C):
                w = [(ws, wv), (sDVE, ev_gg), (sWA, vwo)] if c == 0 else []
                rec.op("tensor", lambda e, c=c, oc=oc, pb=pb:
                       e.matmul(pb[:, 0:TW], wo_sb[:, c * D + oc * 128: c * D + oc * 128 + 128],
                                gg_sb[:, c * TW:(c + 1) * TW],
                                start=(c == 0), stop=(c == C - 1)),
                       waits=w, inc=(sPE if c == C - 1 else None))
            vmm = sPE.n
            rec.op("scalar", lambda e, oc=oc, pb=pb, i=i:
                   e.activation(tmp[i][:, :], pb[:, 0:TW], AF.Identity, bias=col("bo", oc)),
                   waits=[(sPE, vmm), (sDVE, ev_tmp[i]), (sST, st_s_val)], inc=sACT)
            warB[i] = (sACT, sACT.n)
            vev = sACT.n
            rec.op("vector", lambda e, oc=oc, i=i: e.tensor_tensor(
                s_t[:, oc * TW:(oc + 1) * TW], tmp[i][:, :],
                xorig_t[:, oc * TW:(oc + 1) * TW], ALU.add),
                waits=[(sACT, vev), (sLY, vxorig)], inc=sDVE)
            ev_tmp[i] = sDVE.n
        rec.op("gpsimd", lambda e, qt=qt: e.dma_start(
            sv[:, :, qt * TW:(qt + 1) * TW], s_t[:, :].rearrange("p (c n) -> p c n", n=TW)),
            waits=[(sDVE, sDVE.n)], inc=sST, inc_k=16)
        st_s_val = sST.n
    quiesce("A3")
    regA3.close()
    regSH.close()

    regA4 = contextlib.ExitStack()
    sbA4 = lambda name, shape, dt: regA4.enter_context(nc.sbuf_tensor(name, shape, dt))
    hidden = sbA4("hidden", [128, HC * S], BF16)

    regA4a = contextlib.ExitStack()
    sbA4a = lambda name, shape, dt: regA4a.enter_context(nc.sbuf_tensor(name, shape, dt))
    x2_sb = sbA4a("x2_sb", [128, C * S], BF16)
    uA = [sbA4a(f"uA{i}", [128, C * 128], BF16) for i in range(2)]
    uG = [sbA4a(f"uG{i}", [128, C * 128], BF16) for i in range(2)]
    s4_t = sbA4a("s4_t", [128, C * TW], F32)
    xbf4 = sbA4a("xbf4", [128, C * TW], BF16)
    sqbf4 = sbA4a("sqbf4", [128, C * TW], BF16)

    for tt in range(NTT):
        rec.op("sync", lambda e, tt=tt: e.dma_start(
            s4_t[:, :].rearrange("p (c n) -> p c n", n=TW), sv[:, :, tt * TW:(tt + 1) * TW]),
            waits=[(sDVE, sDVE.n)], inc=sLY, inc_k=16)
        vload = sLY.n
        vx2done = ln_tile(lambda c: s4_t[:, c * TW:(c + 1) * TW], ("n2w", "n2b"),
                          [(lambda c, tt=tt: x2_sb[:, c * S + tt * TW: c * S + (tt + 1) * TW], "aff")],
                          xbf4, sqbf4, src_ready=[(sLY, vload)])
        ev_tmp[0] = ev_tmp[1] = vx2done

    for oc in range(C):
        rec.op("gpsimd", lambda e, oc=oc: e.dma_start(
            woutbf_d[oc].rearrange("p (h m) -> p h m", m=128), woutP[oc]),
            inc=sST, inc_k=16)
    st_woutbf = sST.n

    war_ring = [0, 0]
    war_pu = [0, 0, 0, 0]
    ev_tmpA = [0, 0, 0, 0]
    for hc in range(HC):
        i = hc % 2
        rec.op("gpsimd", lambda e, hc=hc, i=i: e.dma_start(
            uA[i][:, :].rearrange("p (k m) -> p k m", m=128), winU[hc]),
            waits=[(sPE, war_ring[i])], inc=sWA, inc_k=16)
        rec.op("gpsimd", lambda e, hc=hc, i=i: e.dma_start(
            uG[i][:, :].rearrange("p (k m) -> p k m", m=128), winG[hc]),
            inc=sWA, inc_k=16)
        vw = sWA.n
        for tt in range(NTT):
            j = tt % 2
            pu = P[0 + j]
            pg = P[2 + j]
            for c in range(C):
                w = [(sWA, vw), (sACT, war_pu[j]), (sDVE, vx2done)] if c == 0 else []
                rec.op("tensor", lambda e, c=c, tt=tt, i=i, pu=pu:
                       e.matmul(pu[:, 0:TW], uA[i][:, c * 128:(c + 1) * 128],
                                x2_sb[:, c * S + tt * TW: c * S + (tt + 1) * TW],
                                start=(c == 0), stop=(c == C - 1)),
                       waits=w)
            for c in range(C):
                w = [(sACT, war_pu[2 + j]), (sDVE, vx2done)] if c == 0 else []
                rec.op("tensor", lambda e, c=c, tt=tt, i=i, pg=pg:
                       e.matmul(pg[:, 0:TW], uG[i][:, c * 128:(c + 1) * 128],
                                x2_sb[:, c * S + tt * TW: c * S + (tt + 1) * TW],
                                start=(c == 0), stop=(c == C - 1)),
                       waits=w, inc=(sPE if c == C - 1 else None))
            vmm = sPE.n
            rec.op("scalar", lambda e, hc=hc, j=j, pu=pu:
                   e.activation(tmp[j][:, :], pu[:, 0:TW], AF.Silu, bias=col("binu", hc)),
                   waits=[(sPE, vmm), (sDVE, ev_tmpA[j])], inc=sACT)
            war_pu[j] = sACT.n
            rec.op("scalar", lambda e, hc=hc, j=j, pg=pg:
                   e.activation(tmp[2 + j][:, :], pg[:, 0:TW], AF.Identity, bias=col("bing", hc)),
                   waits=[(sDVE, ev_tmpA[2 + j])], inc=sACT)
            war_pu[2 + j] = sACT.n
            vev = sACT.n
            rec.op("vector", lambda e, hc=hc, tt=tt, j=j: e.tensor_tensor(
                hidden[:, hc * S + tt * TW: hc * S + (tt + 1) * TW],
                tmp[j][:, :], tmp[2 + j][:, :], ALU.mult),
                waits=[(sACT, vev)], inc=sDVE)
            ev_tmpA[j] = sDVE.n
            ev_tmpA[2 + j] = sDVE.n
        war_ring[i] = sPE.n
    quiesce("A4a")
    regA4a.close()

    regA4b = contextlib.ExitStack()
    sbA4b = lambda name, shape, dt: regA4b.enter_context(nc.sbuf_tensor(name, shape, dt))
    wring = [sbA4b(f"wring{i}", [128, HC * 128], BF16) for i in range(2)]
    s_t2 = sbA4b("s_t2", [128, C * TW], F32)
    xo_t2 = sbA4b("xo_t2", [128, C * TW], F32)
    x3bf_t = sbA4b("x3bf_t", [128, C * TW], BF16)
    xbf5 = sbA4b("xbf5", [128, C * TW], BF16)
    sqbf5 = sbA4b("sqbf5", [128, C * TW], BF16)

    x3v = dview(x3_d)
    x3bv = dview(x3bf_d)
    war_wring = [0, 0]
    war_po = [0, 0]
    war_rps = [0, 0]
    st_x3 = 0
    for tt in range(NTT):
        rec.op("sync", lambda e, tt=tt: e.dma_start(
            s_t2[:, :].rearrange("p (c n) -> p c n", n=TW), sv[:, :, tt * TW:(tt + 1) * TW]),
            waits=[(sDVE, sDVE.n), (sST, st_x3)], inc=sLY, inc_k=16)
        vs = sLY.n
        rec.op("sync", lambda e, tt=tt: e.dma_start(
            xo_t2[:, :].rearrange("p (c n) -> p c n", n=TW), xv[:, :, tt * TW:(tt + 1) * TW]),
            waits=[(sDVE, sDVE.n)], inc=sLY, inc_k=16)
        vxo = sLY.n
        for oc in range(C):
            i = oc % 2
            pb = P[4 + i]
            rec.op("sync", lambda e, oc=oc, i=i: e.dma_start(
                wring[i][:, :], woutbf_d[oc]),
                waits=[(sPE, war_wring[i]), (sST, st_woutbf)], inc=sLX, inc_k=16)
            vw = sLX.n
            for hc in range(HC):
                w = [(sLX, vw), (sACT, war_po[i])] if hc == 0 else []
                rec.op("tensor", lambda e, hc=hc, tt=tt, i=i, pb=pb:
                       e.matmul(pb[:, 0:TW], wring[i][:, hc * 128:(hc + 1) * 128],
                                hidden[:, hc * S + tt * TW: hc * S + (tt + 1) * TW],
                                start=(hc == 0), stop=(hc == HC - 1)),
                       waits=w, inc=(sPE if hc == HC - 1 else None))
            war_wring[i] = sPE.n
            vmm = sPE.n
            rec.op("scalar", lambda e, oc=oc, pb=pb, i=i:
                   e.activation(tmp[i][:, :], pb[:, 0:TW], AF.Identity, bias=col("bout", oc)),
                   waits=[(sPE, vmm), (sDVE, ev_tmp[i])], inc=sACT)
            war_po[i] = sACT.n
            vev = sACT.n
            rec.op("vector", lambda e, oc=oc, i=i: e.tensor_tensor(
                s_t2[:, oc * TW:(oc + 1) * TW], s_t2[:, oc * TW:(oc + 1) * TW],
                tmp[i][:, :], ALU.add),
                waits=[(sACT, vev), (sLY, vs)])
            rec.op("vector", lambda e, oc=oc: e.tensor_tensor(
                xo_t2[:, oc * TW:(oc + 1) * TW], xo_t2[:, oc * TW:(oc + 1) * TW],
                s_t2[:, oc * TW:(oc + 1) * TW], ALU.add),
                waits=[(sLY, vxo)] if oc == 0 else (), inc=sDVE)
            ev_tmp[i] = sDVE.n
        vdone = ln_tile(lambda c: xo_t2[:, c * TW:(c + 1) * TW], None,
                        [(lambda c: s_t2[:, c * TW:(c + 1) * TW], "f32"),
                         (lambda c: x3bf_t[:, c * TW:(c + 1) * TW], "bf16")],
                        xbf5, sqbf5, src_ready=[(sST, st_x3)])
        ev_tmp[0] = ev_tmp[1] = vdone
        rec.op("gpsimd", lambda e, tt=tt: e.dma_start(
            x3v[:, :, tt * TW:(tt + 1) * TW], s_t2[:, :].rearrange("p (c n) -> p c n", n=TW)),
            waits=[(sDVE, vdone)], inc=sST, inc_k=16)
        rec.op("gpsimd", lambda e, tt=tt: e.dma_start(
            x3bv[:, :, tt * TW:(tt + 1) * TW], x3bf_t[:, :].rearrange("p (c n) -> p c n", n=TW)),
            inc=sST, inc_k=16)
        st_x3 = sST.n
        for grp in range(4):
            i = grp % 2
            pb = P[6 + i]
            for c in range(C):
                w = [(sDVE, vdone), (sDVE, war_rps[i])] if c == 0 else []
                rec.op("tensor", lambda e, c=c, grp=grp, pb=pb:
                       e.matmul(pb[0:128, 0:NEX],
                                x3bf_t[:, c * TW + grp * 128: c * TW + grp * 128 + 128],
                                gateWT_bf[:, c * NEX:(c + 1) * NEX],
                                start=(c == 0), stop=(c == C - 1)),
                       waits=w, inc=(sPE if c == C - 1 else None))
            vmm = sPE.n
            vop(lambda e, pb=pb: e.tensor_copy(g_tm[:, :], pb[0:128, 0:NEX]),
                extra=[(sPE, vmm), (sACT, ev_router_act[0])])
            vop(lambda e: e.reduce_max(cc[:, 2:3], g_tm[:, :], axis=AX.X))
            vop(lambda e: e.tensor_single_scalar(g_t2[:, :], g_tm[:, :], cc[:, 2:3], ALU.subtract))
            vop(lambda e: e.tensor_single_scalar(g_t2[:, :], g_t2[:, :], 0.0, ALU.is_ge))
            vop(lambda e: e.scalar_tensor_tensor(g_t1[:, :], g_t2[:, :], -1e30, g_tm[:, :], ALU.mult, ALU.add))
            vop(lambda e: e.reduce_max(cc[:, 3:4], g_t1[:, :], axis=AX.X))
            vop(lambda e: e.tensor_scalar_mul(cc[:, 4:5], cc[:, 2:3], -1.0))
            vd = vop(lambda e: e.tensor_tensor(cc[:, 5:6], cc[:, 2:3], cc[:, 3:4], ALU.subtract))
            vop(lambda e: e.activation(g_t1[:, :], g_tm[:, :], AF.Exp, bias=cc[:, 4:5]),
                extra=[(sDVE, vd)], engine="scalar")
            va = vop(lambda e: e.activation(cc[:, 6:7], cc[:, 5:6], AF.Sigmoid),
                     engine="scalar")
            ev_router_act[0] = va
            vop(lambda e: e.tensor_single_scalar(g_t2[:, :], g_tm[:, :], cc[:, 3:4], ALU.is_ge),
                extra=[(sACT, va)])
            vop(lambda e: e.tensor_tensor(g_tm[:, :], g_t1[:, :], g_t2[:, :], ALU.mult))
            vg = vop(lambda e: e.tensor_scalar_mul(g_tm[:, :], g_tm[:, :], cc[:, 6:7]))
            rec.op("tensor", lambda e, pb=pb: e.transpose(pb[0:NEX, 0:128], g_tm[:, :], ident[:, :]),
                   waits=[(sDVE, vg)], inc=sPE)
            vtr = sPE.n
            rec.op("vector", lambda e, tt=tt, grp=grp, pb=pb: e.tensor_copy(
                g_fm[0:NEX, tt * TW + grp * 128: tt * TW + grp * 128 + 128], pb[0:NEX, 0:128]),
                waits=[(sPE, vtr)], inc=sDVE)
            war_rps[i] = sDVE.n
    quiesce("A4b")
    regA4b.close()
    regA4.close()
    regAW.close()

    regB1 = contextlib.ExitStack()
    sbB1 = lambda name, shape, dt: regB1.enter_context(nc.sbuf_tensor(name, shape, dt))
    W1sb = sbB1("W1sb", [128, K2 * C * 128], BF16)
    W2sb = sbB1("W2sb", [128, C * K2 * 128], BF16)
    sh_sb = sbB1("sh_sb", [128, K2 * TW], BF16)
    x3bf_b = sbB1("x3bf_b", [128, C * TW], BF16)
    acc_t = [sbB1(f"acc_t{i}", [128, TW], F32) for i in range(2)]

    accv = dview(acc_d)
    war_x3bf = 0
    war_sh = 0
    war_ph = [0, 0]
    war_po2 = [0, 0]
    war_ge = 0
    ev_acc = [0, 0]
    st_acc = [[0] * C for _ in range(NTT)]
    st_buf = [0, 0]

    def w1_quarter(E, qtr, wait_pe):
        if E < 2:
            rec.op("gpsimd", lambda e, E=E, qtr=qtr: e.dma_start(
                W1sb[:, qtr * 8192:(qtr + 1) * 8192].rearrange("p (a k m) -> p a k m", a=8, k=C),
                exp1[E, qtr * 8: (qtr + 1) * 8].rearrange("a p k m -> p a k m")),
                waits=[(sPE, wait_pe)], inc=sW1, inc_k=16)
        else:
            rec.op("sync", lambda e, E=E, qtr=qtr: e.dma_start(
                W1sb[:, qtr * 8192:(qtr + 1) * 8192].rearrange("p (a f) -> p a f", a=8),
                exp1bf_d[E, qtr * 8: (qtr + 1) * 8].rearrange("a p f -> p a f")),
                waits=[(sPE, wait_pe), (sST, cast_w1_done[E])], inc=sW1, inc_k=16)

    def w2_quarter(E, qtr, wait_pe):
        if E < 2:
            rec.op("gpsimd", lambda e, E=E, qtr=qtr: e.dma_start(
                W2sb[:, qtr * 8192:(qtr + 1) * 8192].rearrange("p (o k m) -> p o k m", o=2, k=K2),
                exp2[E, qtr * 2: (qtr + 1) * 2].rearrange("o p k m -> p o k m")),
                waits=[(sPE, wait_pe)], inc=sW2, inc_k=16)
        else:
            rec.op("sync", lambda e, E=E, qtr=qtr: e.dma_start(
                W2sb[:, qtr * 8192:(qtr + 1) * 8192].rearrange("p (o f) -> p o f", o=2),
                exp2bf_d[E, qtr * 2: (qtr + 1) * 2].rearrange("o p f -> p o f")),
                waits=[(sPE, wait_pe), (sST, cast_w2_done[E])], inc=sW2, inc_k=16)

    for qtr in range(4):
        w1_quarter(0, qtr, 0)
    vW1 = sW1.n
    for qtr in range(4):
        w2_quarter(0, qtr, 0)
    vW2 = sW2.n
    for E in range(NE if NE_RUN is None else NE_RUN):
        vW1_next = vW1
        vW2_next = vW2
        for tt in range(NTT):
            if E + 2 < NE:
                drip_cast(1)
            rec.op("sync", lambda e, tt=tt: e.dma_start(
                x3bf_b[:, :].rearrange("p (c n) -> p c n", n=TW), x3bv[:, :, tt * TW:(tt + 1) * TW]),
                waits=[(sPE, war_x3bf), (sST, st_x3)], inc=sLX, inc_k=16)
            vx3 = sLX.n
            if E > 0:
                rec.op("tensor", lambda e, E=E, tt=tt: e.matmul(
                    P[4][:, 0:TW], esel_bf[0:NEX, (E - 1) * 128: E * 128],
                    g_fm[0:NEX, tt * TW:(tt + 1) * TW], start=True, stop=True),
                    waits=[(sDVE, war_ge)], inc=sPE)
                vge = sPE.n
                rec.op("vector", lambda e: e.tensor_copy(Ge_sb[:, :], P[4][:, 0:TW]),
                       waits=[(sPE, vge)], inc=sDVE)
                war_ge = sDVE.n
                vge_d = sDVE.n
            else:
                vge_d = 0
            if ge_tap is not None and E == 1:
                rec.op("gpsimd", lambda e, tt=tt: e.dma_start(
                    ge_tap[:, tt * TW:(tt + 1) * TW], Ge_sb[:, :]),
                    waits=[(sDVE, vge_d)], inc=sST, inc_k=16)
            for k2 in range(K2):
                i = k2 % 2
                pb = P[i]
                for c in range(C):
                    w = []
                    if c == 0:
                        w = [(sLX, vx3), (sW1, vW1), (sACT, war_ph[i]), (sPE, war_sh)]
                    rec.op("tensor", lambda e, c=c, k2=k2, pb=pb:
                           e.matmul(pb[:, 0:TW], W1sb[:, k2 * 1024 + c * 128: k2 * 1024 + c * 128 + 128],
                                    x3bf_b[:, c * TW:(c + 1) * TW],
                                    start=(c == 0), stop=(c == C - 1)),
                           waits=w, inc=(sPE if c == C - 1 else None))
                vmm = sPE.n
                rec.op("scalar", lambda e, k2=k2, pb=pb:
                       e.activation(sh_sb[:, k2 * TW:(k2 + 1) * TW], pb[:, 0:TW], AF.Silu),
                       waits=[(sPE, vmm)], inc=sACT)
                war_ph[i] = sACT.n
                if tt == NTT - 1 and E + 1 < NE and (k2 % 8) == 7:
                    w1_quarter(E + 1, k2 // 8, sPE.n)
            if tt == NTT - 1 and E + 1 < NE:
                vW1_next = sW1.n
            vsilu = sACT.n
            war_x3bf = sPE.n
            for oc in range(C):
                i = oc % 2
                pb = P[2 + i]
                if E > 0:
                    rec.op("sync", lambda e, tt=tt, oc=oc, i=i: e.dma_start(
                        acc_t[i][:, :].rearrange("p (a n) -> p a n", a=1),
                        accv[:, oc:oc + 1, tt * TW:(tt + 1) * TW]),
                        waits=[(sDVE, ev_acc[i]),
                               (sST, max(st_acc[tt][oc], st_buf[i]))],
                        inc=sLZ, inc_k=16)
                    vacc = sLZ.n
                for k2 in range(K2):
                    w = []
                    if k2 == 0:
                        w = [(sW2, vW2), (sACT, vsilu), (sDVE, war_po2[i])]
                    rec.op("tensor", lambda e, k2=k2, oc=oc, pb=pb:
                           e.matmul(pb[:, 0:TW], W2sb[:, oc * 4096 + k2 * 128: oc * 4096 + k2 * 128 + 128],
                                    sh_sb[:, k2 * TW:(k2 + 1) * TW],
                                    start=(k2 == 0), stop=(k2 == K2 - 1)),
                           waits=w, inc=(sPE if k2 == K2 - 1 else None))
                vmm = sPE.n
                if E == 0:
                    rec.op("vector", lambda e, i=i, pb=pb: e.tensor_copy(acc_t[i][:, :], pb[:, 0:TW]),
                           waits=[(sPE, vmm), (sST, st_buf[i])], inc=sDVE)
                else:
                    rec.op("vector", lambda e, i=i, pb=pb: e.tensor_tensor(
                        tmp[2 + i][:, :], pb[:, 0:TW], Ge_sb[:, :], ALU.mult),
                        waits=[(sPE, vmm), (sDVE, vge_d), (sLZ, vacc)])
                    rec.op("vector", lambda e, i=i: e.tensor_tensor(
                        acc_t[i][:, :], acc_t[i][:, :], tmp[2 + i][:, :], ALU.add),
                        inc=sDVE)
                war_po2[i] = sDVE.n
                ev_acc[i] = sDVE.n
                vadd = sDVE.n
                rec.op("gpsimd", lambda e, tt=tt, oc=oc, i=i: e.dma_start(
                    accv[:, oc:oc + 1, tt * TW:(tt + 1) * TW],
                    acc_t[i][:, :].rearrange("p (a n) -> p a n", a=1)),
                    waits=[(sDVE, vadd)], inc=sST, inc_k=16)
                st_acc[tt][oc] = sST.n
                st_buf[i] = sST.n
                if tt == NTT - 1 and E + 1 < NE and (oc % 2) == 1:
                    w2_quarter(E + 1, oc // 2, sPE.n)
            if tt == NTT - 1 and E + 1 < NE:
                vW2_next = sW2.n
            war_sh = sPE.n
        vW1 = vW1_next
        vW2 = vW2_next
    quiesce("B1")
    regB1.close()

    regB2 = contextlib.ExitStack()
    sbB2 = lambda name, shape, dt: regB2.enter_context(nc.sbuf_tensor(name, shape, dt))
    accB = [sbB2(f"accB{i}", [128, C * TW], F32) for i in range(2)]
    x3B = [sbB2(f"x3B{i}", [128, C * TW], F32) for i in range(2)]
    z2B = sbB2("z2B", [128, C * TW], F32)
    outB = sbB2("outB", [128, C * TW], F32)
    xbf6 = sbB2("xbf6", [128, C * TW], BF16)
    sqbf6 = sbB2("sqbf6", [128, C * TW], BF16)

    outv = dview(out_d)
    st_out = 0
    vldB = [0] * NTT
    zwar = [0, 0]

    def b2_load(tt):
        rec.op("sync", lambda e, tt=tt: e.dma_start(
            accB[tt % 2][:, :].rearrange("p (c n) -> p c n", n=TW), accv[:, :, tt * TW:(tt + 1) * TW]),
            waits=[(sDVE, zwar[tt % 2])], inc=sLX, inc_k=16)
        rec.op("sync", lambda e, tt=tt: e.dma_start(
            x3B[tt % 2][:, :].rearrange("p (c n) -> p c n", n=TW), x3v[:, :, tt * TW:(tt + 1) * TW]),
            inc=sLX, inc_k=16)
        vldB[tt] = sLX.n

    b2_load(0)
    b2_load(1)
    for tt in range(NTT):
        for c in range(C):
            w = [(sLX, vldB[tt])] if c == 0 else []
            rec.op("vector", lambda e, c=c, tt=tt: e.tensor_tensor(
                z2B[:, c * TW:(c + 1) * TW], x3B[tt % 2][:, c * TW:(c + 1) * TW],
                accB[tt % 2][:, c * TW:(c + 1) * TW], ALU.add),
                waits=w, inc=(sDVE if c == C - 1 else None))
        zwar[tt % 2] = sDVE.n
        if tt + 2 < NTT:
            b2_load(tt + 2)
        vz = sDVE.n
        vdone = ln_tile(lambda c: z2B[:, c * TW:(c + 1) * TW], None,
                        [(lambda c: outB[:, c * TW:(c + 1) * TW], "f32")],
                        xbf6, sqbf6, src_ready=[(sDVE, vz)])
        rec.op("gpsimd", lambda e, tt=tt: e.dma_start(
            outv[:, :, tt * TW:(tt + 1) * TW], outB[:, :].rearrange("p (c n) -> p c n", n=TW)),
            waits=[(sDVE, vdone), (sST, st_out)], inc=sST, inc_k=16)
        st_out = sST.n

    if gfm_tap is not None:
        rec.op("gpsimd", lambda e: e.dma_start(gfm_tap[:, :], g_fm[0:NEX, :]),
               waits=[(sST, sST.n)], inc=sST, inc_k=16)
    for (tap_d, src_d) in tap_outs:
        rec.op("gpsimd", lambda e, tap_d=tap_d, src_d=src_d: e.dma_start(
            tap_d[:, :], src_d[:, :]), waits=[(sST, sST.n)], inc=sST, inc_k=16)

    quiesce("B2")

    with nc.Block() as block:
        block.gpsimd(lambda e: e.wait_ge(sST.h, sST.n))

    return nc


def _pack_inputs(inputs):
    f = np.float32
    A = lambda v: np.ascontiguousarray(np.asarray(v, dtype=f))

    x = A(inputs["L0_attn_x"])
    Wq = A(inputs["L0_attn_wq"]); Wk = A(inputs["L0_attn_wk"])
    Wv = A(inputs["L0_attn_wv"]); Wg = A(inputs["L0_attn_wg"])
    Wo = A(inputs["L0_attn_wo"])
    w_in = A(inputs["L0_attn_w_in"])
    w_out = A(inputs["L0_attn_w_out"])
    sh1 = A(inputs["L0_ffn_shared_w1"])
    sh2 = A(inputs["L0_ffn_shared_w2"])
    ew1 = A(inputs["L0_ffn_experts_w1"])
    ew2 = A(inputs["L0_ffn_experts_w2"])
    gw = A(inputs["L0_ffn_gate_w"])

    wu = np.zeros((D, HIDP), f); wu[:, :HID] = w_in[:, :HID]
    wgg = np.zeros((D, HIDP), f); wgg[:, :HID] = w_in[:, HID:]
    winU = np.ascontiguousarray(wu.reshape(C, 128, HC, 128).transpose(2, 1, 0, 3))
    winG = np.ascontiguousarray(wgg.reshape(C, 128, HC, 128).transpose(2, 1, 0, 3))
    wop = np.zeros((HIDP, D), f); wop[:HID] = w_out
    woutP = np.ascontiguousarray(wop.reshape(HC, 128, C, 128).transpose(2, 1, 0, 3))

    e1 = np.empty((NE, K2, 128, C, 128), f)
    e2 = np.empty((NE, C, 128, K2, 128), f)
    for E in range(NE):
        w1 = sh1 if E == 0 else ew1[E - 1]
        w2 = sh2 if E == 0 else ew2[E - 1]
        w1T = np.ascontiguousarray(w1.T)
        w2T = np.ascontiguousarray(w2.T)
        e1[E] = w1T.reshape(C, 128, K2, 128).transpose(2, 1, 0, 3)
        e2[E] = w2T.reshape(K2, 128, C, 128).transpose(2, 1, 0, 3)

    gateWT = np.ascontiguousarray(gw.T)

    pc = np.zeros((128, PCOLS_W), f)
    def setc(name, vec, n):
        pc[:, PCOL_OFF[name]:PCOL_OFF[name] + n] = np.asarray(vec, f).reshape(n, 128).T
    setc("n1w", inputs["L0_attn_norm1_w"], 8); setc("n1b", inputs["L0_attn_norm1_b"], 8)
    setc("n2w", inputs["L0_attn_norm2_w"], 8); setc("n2b", inputs["L0_attn_norm2_b"], 8)
    setc("gnw", inputs["L0_attn_gn_w"], 8); setc("gnb", inputs["L0_attn_gn_b"], 8)
    setc("bq", inputs["L0_attn_bq"], 8); setc("bk", inputs["L0_attn_bk"], 8)
    setc("bg", inputs["L0_attn_bg"], 8); setc("bo", inputs["L0_attn_bo"], 8)
    setc("bout", inputs["L0_attn_b_out"], 8)
    b_in = A(inputs["L0_attn_b_in"])
    biu = np.zeros(HIDP, f); biu[:HID] = b_in[:HID]
    big = np.zeros(HIDP, f); big[:HID] = b_in[HID:]
    setc("binu", biu, HC); setc("bing", big, HC)
    gam = 1.0 - 2.0 ** (-5.0 - np.arange(H, dtype=np.float64))
    lng = np.log(gam)
    pp, ccx = np.meshgrid(np.arange(128), np.arange(8), indexing="ij")
    hmap = 2 * ccx + pp // 64
    pc[:, PCOL_OFF["lng"]:PCOL_OFF["lng"] + 8] = lng[hmap].astype(f)
    pc[:, PCOL_OFF["nlng"]:PCOL_OFF["nlng"] + 8] = (-lng[hmap]).astype(f)

    m, n = np.meshgrid(np.arange(128), np.arange(TW), indexing="ij")
    masks = np.concatenate([(n - m >= 128 * j).astype(f) for j in range(4)], axis=1)
    masks = np.ascontiguousarray(masks)
    R2 = np.broadcast_to(np.arange(TW, dtype=f), (128, TW)).copy()
    ident = np.eye(128, dtype=f)
    esel = np.zeros((NEX, NEX * 128), f)
    for e in range(NEX):
        esel[e, e * 128:(e + 1) * 128] = 1.0
    bv_row = A(inputs["L0_attn_bv"]).reshape(1, D)
    sel2 = np.zeros((2, 128), f)
    sel2[0, 0:64] = 1.0
    sel2[1, 64:128] = 1.0
    selT = np.ascontiguousarray(sel2.T)

    shared = dict(Wq=Wq, Wk=Wk, Wv=Wv, Wg=Wg, Wo=Wo, winU=winU, winG=winG,
                  woutP=woutP, exp1=e1, exp2=e2, gateWT=gateWT, pcols_in=pc,
                  masks_in=masks, R2_in=R2, ident_in=ident, esel_in=esel, bv_row=bv_row,
                  sel2_in=sel2, selT_in=selT)
    in_maps = []
    for b in range(8):
        m_ = dict(shared)
        m_["x_in"] = np.ascontiguousarray(x[b].T)
        in_maps.append(m_)
    return in_maps


def kernel(**inputs):
    in_maps = _pack_inputs(inputs)
    nc = build_program()
    res = run_bass_kernel_spmd(nc, in_maps, core_ids=list(range(8)))
    out = np.stack([np.asarray(r["out"]).T for r in res.results], axis=0)
    return np.ascontiguousarray(out.astype(np.float32))


# version 12
# speedup vs baseline: 1.2003x; 1.0015x over previous
import sys
sys.path.insert(0, '/opt/trn_rl_repo')
import contextlib
import numpy as np

import concourse.bass as bass
import concourse.bass_utils as _bass_utils
import concourse.mybir as mybir
from concourse.bass_utils import run_bass_kernel_spmd

if not getattr(_bass_utils, "_ldw_opt_patched", False):
    _orig_run_command = _bass_utils.run_command

    def _run_command_ldw(cmd, *a, **kw):
        if isinstance(cmd, list):
            cmd = ["--enable-ldw-opt=true" if c == "--enable-ldw-opt=false" else c
                   for c in cmd]
        return _orig_run_command(cmd, *a, **kw)

    _bass_utils.run_command = _run_command_ldw
    _bass_utils._ldw_opt_patched = True

F32 = mybir.dt.float32
BF16 = mybir.dt.bfloat16
AF = mybir.ActivationFunctionType
ALU = mybir.AluOpType
AX = mybir.AxisListType

D = 1024
S = 2048
C = 8
NTT = 4
TW = 512
H = 16
HD = 64
HID = 2730
HIDP = 2816
HC = 22
MH = 4096
K2 = 32
NE = 9
NEX = 8
EPS = 1e-5
LN8 = float(np.log(8.0))

_PCOLS = [("n1w", 8), ("n1b", 8), ("n2w", 8), ("n2b", 8), ("gnw", 8), ("gnb", 8),
          ("bq", 8), ("bk", 8), ("bg", 8), ("bo", 8), ("bout", 8),
          ("binu", HC), ("bing", HC), ("lng", 8), ("nlng", 8)]
PCOL_OFF = {}
_o = 0
for _n, _k in _PCOLS:
    PCOL_OFF[_n] = _o
    _o += _k
PCOLS_W = _o


class Sem:
    def __init__(self, handle):
        self.h = handle
        self.n = 0


class Rec:
    ENGINES = ("tensor", "vector", "scalar", "gpsimd", "sync")

    def __init__(self, nc):
        self.nc = nc
        self.ev = {e: [] for e in self.ENGINES}

    def op(self, eng, emitfn, waits=(), inc=None, inc_k=1):
        if inc is not None:
            inc.n += inc_k
        waits = tuple((s.h, v) for (s, v) in waits if v > 0)
        ih = None if inc is None else inc.h

        def f(e, emitfn=emitfn, waits=waits, ih=ih, inc_k=inc_k):
            for (h, v) in waits:
                e.wait_ge(h, v)
            ins = emitfn(e)
            if ih is not None:
                ins.then_inc(ih, inc_k)
        self.ev[eng].append(f)

    def wait(self, eng, sem, v):
        if v > 0:
            self.ev[eng].append(lambda e, h=sem.h, v=v: e.wait_ge(h, v))

    def flush(self, scope=None):
        ev = self.ev
        self.ev = {e: [] for e in self.ENGINES}
        import contextlib as _cl
        cm = self.nc.named_scope(scope) if scope else _cl.nullcontext()
        with cm:
            with self.nc.Block() as block:
                for name, reg in (("tensor", block.tensor), ("vector", block.vector),
                                  ("scalar", block.scalar), ("gpsimd", block.gpsimd),
                                  ("sync", block.sync)):
                    if ev[name]:
                        reg(lambda e, fs=tuple(ev[name]): [f(e) for f in fs])


NE_RUN = None

def build_program(debug_taps=()):
    nc = bass.Bass("TRN2", target_bir_lowering=False)
    ctx = contextlib.ExitStack()

    di = lambda name, shape: nc.dram_tensor(name, shape, F32, kind="ExternalInput")
    x_in = di("x_in", [D, S])
    Wq = di("Wq", [D, D]); Wk = di("Wk", [D, D]); Wv = di("Wv", [D, D])
    Wg = di("Wg", [D, D]); Wo = di("Wo", [D, D])
    winU = di("winU", [HC, 128, C, 128])
    winG = di("winG", [HC, 128, C, 128])
    woutP = di("woutP", [C, 128, HC, 128])
    exp1 = di("exp1", [NE, K2, 128, C, 128])
    exp2 = di("exp2", [NE, C, 128, K2, 128])
    gateWT_in = di("gateWT", [D, NEX])
    pcols_in = di("pcols_in", [128, PCOLS_W])
    masks_in = di("masks_in", [128, 4 * TW])
    R2_in = di("R2_in", [128, TW])
    ident_in = di("ident_in", [128, 128])
    esel_in = di("esel_in", [NEX, NEX * 128])
    sel2_in = di("sel2_in", [2, 128])
    selT_in = di("selT_in", [128, 2])
    bv_in = di("bv_row", [1, D])

    out_d = nc.dram_tensor("out", [D, S], F32, kind="ExternalOutput")

    dsc = lambda name, dt: nc.dram_tensor(name, [D, S], dt, kind="Internal")
    q_d = dsc("q_d", BF16)
    gate_d = dsc("gate_d", BF16)
    s_d = dsc("s_d", F32)
    woutbf_d = nc.dram_tensor("woutbf_d", [C, 128, HC * 128], BF16, kind="Internal")
    exp1bf_d = nc.dram_tensor("exp1bf_d", [NE, K2, 128, C * 128], BF16, kind="Internal")
    exp2bf_d = nc.dram_tensor("exp2bf_d", [NE, C, 128, K2 * 128], BF16, kind="Internal")
    x3_d = dsc("x3_d", F32)
    x3bf_d = dsc("x3bf_d", BF16)
    acc_d = dsc("acc_d", F32)

    tap_outs = []
    gfm_tap = None
    ge_tap = None
    for t in debug_taps:
        if t == "ge":
            ge_tap = nc.dram_tensor("tap_ge", [128, S], F32, kind="ExternalOutput")
            continue
        if t == "gfm":
            gfm_tap = nc.dram_tensor("tap_gfm", [NEX, S], BF16, kind="ExternalOutput")
            continue
        src = {"xn": xn_d, "q": q_d, "gate": gate_d, "s": s_d, "x3": x3_d,
               "x3bf": x3bf_d, "acc": acc_d}[t]
        dt = BF16 if t in ("xn", "q", "gate", "x3bf") else F32
        tap_outs.append((nc.dram_tensor("tap_" + t, [D, S], dt, kind="ExternalOutput"), src))

    def dview(d):
        return d[:, :].rearrange("(c p) t -> p c t", p=128)

    sems = {}
    def mksem(name):
        sems[name] = Sem(ctx.enter_context(nc.semaphore(name)))
        return sems[name]
    sPE = mksem("sPE"); sACT = mksem("sACT"); sDVE = mksem("sDVE")
    sLX = mksem("sLX")
    sLY = mksem("sLY")
    sLZ = mksem("sLZ")
    sWA = mksem("sWA")
    sW1 = mksem("sW1"); sW2 = mksem("sW2")
    sST = mksem("sST")
    ALL_DMA = (sLX, sLY, sLZ, sWA, sW1, sW2, sST)

    sb_root = lambda name, shape, dt: ctx.enter_context(nc.sbuf_tensor(name, shape, dt))
    pcols = sb_root("pcols", [128, PCOLS_W], F32)
    esel_bf = sb_root("esel_bf", [NEX, NEX * 128], BF16)
    g_ones = sb_root("g_ones", [1, S], BF16)
    g_fm = sb_root("g_fm", [NEX, S], BF16)
    g_tm = sb_root("g_tm", [128, NEX], F32)
    g_t1 = sb_root("g_t1", [128, NEX], F32)
    g_t2 = sb_root("g_t2", [128, NEX], F32)
    ones_col = sb_root("ones_col", [128, 1], BF16)
    ones_row = sb_root("ones_row", [1, 128], BF16)
    eps_col = sb_root("eps_col", [128, 1], F32)
    cc = sb_root("cc", [128, 8], F32)
    r_mu = sb_root("r_mu", [2, TW], F32)
    r_var = sb_root("r_var", [2, TW], F32)
    r_r = sb_root("r_r", [2, TW], F32)
    r_u = sb_root("r_u", [2, TW], F32)
    r_rbf = sb_root("r_rbf", [2, TW], BF16)
    r_ubf = sb_root("r_ubf", [2, TW], BF16)
    tmp = [sb_root(f"tmp{i}", [128, TW], F32) for i in range(4)]
    Ge_sb = sb_root("Ge_sb", [128, TW], F32)

    P = [ctx.enter_context(nc.psum_tensor(f"p{i}", [128, 512], F32)) for i in range(8)]

    rec = Rec(nc)
    def col(name, i=0, r0=0, r1=128):
        off = PCOL_OFF[name] + i
        return pcols[r0:r1, off:off + 1]

    def quiesce(scope=None):
        for eng, ss in (("gpsimd", (sWA, sW1, sW2, sST)), ("sync", (sLX, sLY, sLZ))):
            for s in ss:
                rec.wait(eng, s, s.n)
        rec.flush(scope)

    def ph0(g):
        for s in sems.values():
            g.sem_clear(s.h)
        g.memset(ones_col[:, :], 1.0)
        g.memset(ones_row[:, :], 1.0)
        g.memset(g_ones[:, :], 1.0)
        g.memset(eps_col[:, :], EPS)
    rec.ev["gpsimd"].append(ph0)
    rec.flush()
    rec.op("gpsimd", lambda e: e.dma_start(esel_bf[:, :], esel_in[:, :]),
           inc=sWA, inc_k=16)
    rec.op("sync", lambda e: e.dma_start(pcols[:, :], pcols_in[:, :]),
           inc=sLX, inc_k=16)
    quiesce("init1")

    regAW = ctx.enter_context(contextlib.ExitStack())
    sbAW = lambda name, shape, dt: regAW.enter_context(nc.sbuf_tensor(name, shape, dt))
    masks_bf = sbAW("masks_bf", [128, 4 * TW], BF16)
    R2 = sbAW("R2", [128, TW], F32)
    qsc = [sbAW(f"qsc{i}", [128, TW], F32) for i in range(2)]
    sel2_bf = sbAW("sel2_bf", [2, 128], BF16)
    selT_bf = sbAW("selT_bf", [128, 2], BF16)
    bv_bf = sbAW("bv_bf", [1, D], BF16)
    gateWT_bf = sbAW("gateWT_bf", [128, C * NEX], BF16)
    ident = sbAW("ident", [128, 128], F32)

    rec.op("gpsimd", lambda e: e.dma_start(masks_bf[:, :], masks_in[:, :]),
           inc=sWA, inc_k=16)
    rec.op("gpsimd", lambda e: e.dma_start(bv_bf[:, :], bv_in[:, :]),
           inc=sWA, inc_k=16)
    rec.op("gpsimd", lambda e: e.dma_start(
        gateWT_bf[:, :].rearrange("p (c e) -> p c e", e=NEX),
        gateWT_in[:, :].rearrange("(c p) e -> p c e", p=128)),
        inc=sWA, inc_k=16)
    rec.op("gpsimd", lambda e: e.dma_start(sel2_bf[:, :], sel2_in[:, :]), inc=sWA, inc_k=16)
    rec.op("gpsimd", lambda e: e.dma_start(selT_bf[:, :], selT_in[:, :]), inc=sWA, inc_k=16)
    rec.op("sync", lambda e: e.dma_start(R2[:, :], R2_in[:, :]), inc=sLX, inc_k=16)
    rec.op("sync", lambda e: e.dma_start(ident[:, :], ident_in[:, :]), inc=sLX, inc_k=16)
    quiesce("init2")

    def vop(emit, extra=(), engine="vector", sem=None):
        sem = sem or (sDVE if engine == "vector" else sACT)
        prev = sem.n
        rec.op(engine, emit, waits=list(extra) + [(sem, prev)], inc=sem)
        return sem.n

    def ln_rows(inv_n, pA, pB, vstat, nr=1):
        vop(lambda e: e.tensor_scalar_mul(r_mu[0:nr, :], pA[0:nr, 0:TW], inv_n),
            extra=[(sPE, vstat)])
        vop(lambda e: e.tensor_scalar_mul(r_var[0:nr, :], pB[0:nr, 0:TW], inv_n))
        vop(lambda e: e.tensor_tensor(r_u[0:nr, :], r_mu[0:nr, :], r_mu[0:nr, :], ALU.mult))
        vv = vop(lambda e: e.tensor_tensor(r_var[0:nr, :], r_var[0:nr, :], r_u[0:nr, :], ALU.subtract))
        vop(lambda e: e.activation(r_var[0:nr, :], r_var[0:nr, :], AF.Ln,
                                   bias=eps_col[0:nr, 0:1]),
            extra=[(sDVE, vv)], engine="scalar")
        va = vop(lambda e: e.activation(r_r[0:nr, :], r_var[0:nr, :], AF.Exp, scale=-0.5),
                 engine="scalar")
        vop(lambda e: e.tensor_tensor(r_u[0:nr, :], r_mu[0:nr, :], r_r[0:nr, :], ALU.mult),
            extra=[(sACT, va)])
        vop(lambda e: e.tensor_copy(r_rbf[0:nr, :], r_r[0:nr, :]))
        vop(lambda e: e.tensor_copy(r_ubf[0:nr, :], r_u[0:nr, :]))
        return sDVE.n

    def ln_tile(src_fn, aff, writers, x_bf, sq_bf, src_ready=(), n_feat=D):
        for c in range(C):
            w = list(src_ready) if c == 0 else []
            rec.op("vector", lambda e, c=c: e.tensor_copy(x_bf[:, c * TW:(c + 1) * TW], src_fn(c)),
                   waits=w, inc=(sDVE if c == C - 1 else None))
        vcast = sDVE.n
        for c in range(C):
            rec.op("scalar", lambda e, c=c: e.activation(sq_bf[:, c * TW:(c + 1) * TW],
                                                         x_bf[:, c * TW:(c + 1) * TW], AF.Square),
                   waits=[(sDVE, vcast)] if c == 0 else (), inc=(sACT if c == C - 1 else None))
        vsq = sACT.n
        for c in range(C):
            rec.op("tensor", lambda e, c=c: e.matmul(P[2][0:1, 0:TW], ones_col[:, 0:1],
                                                     x_bf[:, c * TW:(c + 1) * TW],
                                                     start=(c == 0), stop=(c == C - 1)),
                   waits=[(sDVE, vcast)] if c == 0 else ())
        for c in range(C):
            rec.op("tensor", lambda e, c=c: e.matmul(P[3][0:1, 0:TW], ones_col[:, 0:1],
                                                     sq_bf[:, c * TW:(c + 1) * TW],
                                                     start=(c == 0), stop=(c == C - 1)),
                   waits=[(sACT, vsq)] if c == 0 else (),
                   inc=(sPE if c == C - 1 else None))
        vrows = ln_rows(1.0 / n_feat, P[2], P[3], sPE.n)
        rec.op("tensor", lambda e: e.matmul(P[4][:, 0:TW], ones_row[0:1, :], r_rbf[0:1, :],
                                            start=True, stop=True),
               waits=[(sDVE, vrows)])
        rec.op("tensor", lambda e: e.matmul(P[5][:, 0:TW], ones_row[0:1, :], r_ubf[0:1, :],
                                            start=True, stop=True), inc=sPE)
        vbc = sPE.n
        for c in range(C):
            tm = tmp[c % 2]
            rec.op("vector", lambda e, c=c, tm=tm: e.tensor_tensor(tm[:, :], src_fn(c),
                                                                   P[4][:, 0:TW], ALU.mult),
                   waits=[(sPE, vbc)] if c == 0 else ())
            rec.op("vector", lambda e, tm=tm: e.tensor_tensor(tm[:, :], tm[:, :],
                                                              P[5][:, 0:TW], ALU.subtract))
            for wi, (dst_fn, mode) in enumerate(writers):
                last = (c == C - 1 and wi == len(writers) - 1)
                if mode == "aff":
                    wname, bname = aff
                    rec.op("vector", lambda e, c=c, tm=tm, dst_fn=dst_fn, wname=wname, bname=bname:
                           e.tensor_scalar(dst_fn(c), tm[:, :], col(wname, c), col(bname, c),
                                           ALU.mult, ALU.add),
                           inc=(sDVE if last else None))
                else:
                    rec.op("vector", lambda e, c=c, tm=tm, dst_fn=dst_fn:
                           e.tensor_copy(dst_fn(c), tm[:, :]),
                           inc=(sDVE if last else None))
        return sDVE.n

    ev_tmp = [0, 0]
    ev_router_act = [0]

    cast_w1_done = [0] * NE
    cast_w2_done = [0] * NE
    _cast_q = []
    for E in range(2, NE):
        for half in range(2):
            _cast_q.append(("w1", E, half))
        for half in range(2):
            _cast_q.append(("w2", E, half))

    def drip_cast(n=1):
        for _ in range(n):
            if not _cast_q:
                return
            kind, E, half = _cast_q.pop(0)
            if kind == "w1":
                rec.op("gpsimd", lambda e, E=E, half=half: e.dma_start(
                    exp1bf_d[E, half * 16:(half + 1) * 16],
                    exp1[E, half * 16:(half + 1) * 16].rearrange("a p k m -> a p (k m)")),
                    inc=sST, inc_k=16)
                if half == 1:
                    cast_w1_done[E] = sST.n
            else:
                rec.op("gpsimd", lambda e, E=E, half=half: e.dma_start(
                    exp2bf_d[E, half * 4:(half + 1) * 4],
                    exp2[E, half * 4:(half + 1) * 4].rearrange("a p k m -> a p (k m)")),
                    inc=sST, inc_k=16)
                if half == 1:
                    cast_w2_done[E] = sST.n

    regSH = ctx.enter_context(contextlib.ExitStack())
    sbSH = lambda name, shape, dt: regSH.enter_context(nc.sbuf_tensor(name, shape, dt))
    k_sb = sbSH("k_sb", [128, C * S], BF16)
    v_sb = sbSH("v_sb", [128, 16 * D], BF16)
    gate_t = sbSH("gate_t", [128, C * TW], BF16)
    wo_sb = sbSH("wo_sb", [128, C * D], BF16)
    q_t = sbSH("q_t", [128, C * TW], BF16)

    regA12 = contextlib.ExitStack()
    sbA12 = lambda name, shape, dt: regA12.enter_context(nc.sbuf_tensor(name, shape, dt))
    xn_sb = sbA12("xn_sb", [128, C * S], BF16)

    x_war = [0, 0]
    regA12a = contextlib.ExitStack()
    sbA12a = lambda name, shape, dt: regA12a.enter_context(nc.sbuf_tensor(name, shape, dt))
    x_t = [sbA12a("x_t0", [128, C * TW], F32)]
    xbf_t = sbA12a("xbf_t", [128, C * TW], BF16)
    sqbf_t = sbA12a("sqbf_t", [128, C * TW], BF16)

    xv = dview(x_in)
    for tt in range(NTT):
        xt = x_t[0]
        rec.op("sync", lambda e, tt=tt, xt=xt: e.dma_start(
            xt[:, :].rearrange("p (c n) -> p c n", n=TW), xv[:, :, tt * TW:(tt + 1) * TW]),
            waits=[(sDVE, x_war[0])], inc=sLX, inc_k=16)
        vload = sLX.n
        vdone = ln_tile(lambda c, xt=xt: xt[:, c * TW:(c + 1) * TW], ("n1w", "n1b"),
                        [(lambda c, tt=tt: xn_sb[:, c * S + tt * TW: c * S + (tt + 1) * TW], "aff")],
                        xbf_t, sqbf_t, src_ready=[(sLX, vload)])
        ev_tmp[0] = ev_tmp[1] = vdone
        x_war[0] = vdone
    v_ln1 = sDVE.n
    quiesce("A12a")
    regA12a.close()

    regA12b = contextlib.ExitStack()
    sbA12b = lambda name, shape, dt: regA12b.enter_context(nc.sbuf_tensor(name, shape, dt))
    wbuf = [sbA12b(f"wbuf{i}", [128, C * D], BF16) for i in range(2)]

    qv = dview(q_d)
    gv = dview(gate_d)
    st_gate = [0] * NTT

    def proj_weight_load(W, buf, after_pe):
        rec.op("gpsimd", lambda e, W=W, buf=buf: e.dma_start(
            buf[:, :].rearrange("p (k m) -> p k m", m=D),
            W[:, :].rearrange("(k p) m -> p k m", p=128)),
            waits=[(sPE, after_pe)], inc=sWA, inc_k=16)
        return sWA.n

    wbuf_war = [0, 0]
    ev_qsc = [0, 0]
    ev_cc = [0, 0]
    psum_war = [0, 0]

    projs = (("q", Wq), ("k", Wk), ("v", Wv), ("g", Wg))
    vw_all = []
    for pi, (proj, W) in enumerate(projs[:2]):
        vw_all.append(proj_weight_load(W, wbuf[pi % 2], wbuf_war[pi % 2]))
    for pi, (proj, W) in enumerate(projs):
        vw = vw_all[pi]
        wb = wbuf[pi % 2]
        for tt in range(NTT):
            vxn = v_ln1
            xn_l = xn_sb[:, :]
            if proj in ("q", "k"):
                for oc in range(C):
                    i = oc % 2
                    pb = P[6 + i]
                    lncol = "lng" if proj == "q" else "nlng"
                    bias2 = -LN8 if proj == "q" else 0.0
                    rec.op("vector", lambda e, oc=oc, i=i, lncol=lncol, bias2=bias2, tt=tt:
                           e.tensor_scalar(cc[:, i:i + 1], col(lncol, oc), float(tt * TW), bias2,
                                           ALU.mult, ALU.add),
                           waits=[(sACT, ev_cc[i])], inc=sDVE)
                    vb = sDVE.n
                    rec.op("scalar", lambda e, oc=oc, i=i, lncol=lncol:
                           e.activation(qsc[i][:, :], R2[:, :], AF.Exp,
                                        bias=cc[:, i:i + 1], scale=col(lncol, oc)),
                           waits=[(sDVE, max(vb, ev_qsc[i]))], inc=sACT)
                    ev_cc[i] = sACT.n
                    vgen = sACT.n
                    for c in range(C):
                        w = [(sDVE, vxn), (sWA, vw), (sACT, psum_war[i])] if c == 0 else []
                        rec.op("tensor", lambda e, c=c, oc=oc, tt=tt, pb=pb, wb=wb:
                               e.matmul(pb[:, 0:TW], wb[:, c * D + oc * 128: c * D + oc * 128 + 128],
                                        xn_sb[:, c * S + tt * TW: c * S + (tt + 1) * TW],
                                        start=(c == 0), stop=(c == C - 1)),
                               waits=w, inc=(sPE if c == C - 1 else None))
                    vmm = sPE.n
                    bcol = "bq" if proj == "q" else "bk"
                    rec.op("scalar", lambda e, pb=pb, i=i, oc=oc, bcol=bcol:
                           e.activation(tmp[i][:, :], pb[:, 0:TW], AF.Identity, bias=col(bcol, oc)),
                           waits=[(sPE, vmm), (sDVE, ev_tmp[i])], inc=sACT)
                    psum_war[i] = sACT.n
                    vev = sACT.n
                    dst_sb = q_t if proj == "q" else k_sb
                    if proj == "q":
                        dsl = lambda e, oc=oc: q_t[:, oc * TW:(oc + 1) * TW]
                    else:
                        dsl = lambda e, oc=oc, tt=tt: k_sb[:, oc * S + tt * TW: oc * S + (tt + 1) * TW]
                    rec.op("vector", lambda e, i=i, dsl=dsl:
                           e.tensor_tensor(dsl(e), tmp[i][:, :], qsc[i][:, :], ALU.mult),
                           waits=[(sACT, vgen), (sACT, vev)], inc=sDVE)
                    ev_tmp[i] = sDVE.n
                    ev_qsc[i] = sDVE.n
                if proj == "q":
                    rec.op("gpsimd", lambda e, tt=tt: e.dma_start(
                        qv[:, :, tt * TW:(tt + 1) * TW],
                        q_t[:, :].rearrange("p (c n) -> p c n", n=TW)),
                        waits=[(sDVE, sDVE.n)], inc=sST, inc_k=16)
                    rec.wait("vector", sST, sST.n)
            elif proj == "v":
                for grp in range(4):
                    gg_ = tt * 4 + grp
                    for half in range(2):
                        i = (grp * 2 + half) % 2
                        pb = P[6 + i]
                        for c in range(C):
                            w = [(sDVE, vxn), (sWA, vw), (sACT, psum_war[i])] if c == 0 else []
                            rec.op("tensor", lambda e, c=c, grp=grp, half=half, tt=tt, pb=pb, wb=wb:
                                   e.matmul(pb[:, 0:TW],
                                            xn_sb[:, c * S + tt * TW + grp * 128: c * S + tt * TW + grp * 128 + 128],
                                            wb[:, c * D + half * 512: c * D + (half + 1) * 512],
                                            start=(c == 0), stop=False),
                                   waits=w)
                        rec.op("tensor", lambda e, half=half, pb=pb:
                               e.matmul(pb[:, 0:TW], ones_row[0:1, :],
                                        bv_bf[0:1, half * 512:(half + 1) * 512],
                                        start=False, stop=True), inc=sPE)
                        vmm = sPE.n
                        rec.op("scalar", lambda e, gg_=gg_, half=half, pb=pb:
                               e.activation(v_sb[:, gg_ * D + half * 512: gg_ * D + (half + 1) * 512],
                                            pb[:, 0:TW], AF.Copy),
                               waits=[(sPE, vmm)], inc=sACT)
                        psum_war[i] = sACT.n
            else:
                for oc in range(C):
                    i = oc % 2
                    pb = P[6 + i]
                    for c in range(C):
                        w = [(sDVE, vxn), (sWA, vw), (sACT, psum_war[i])] if c == 0 else []
                        if c == 0 and tt > 0:
                            w.append((sST, st_gate[tt - 1]))
                        rec.op("tensor", lambda e, c=c, oc=oc, tt=tt, pb=pb, wb=wb:
                               e.matmul(pb[:, 0:TW], wb[:, c * D + oc * 128: c * D + oc * 128 + 128],
                                        xn_sb[:, c * S + tt * TW: c * S + (tt + 1) * TW],
                                        start=(c == 0), stop=(c == C - 1)),
                               waits=w, inc=(sPE if c == C - 1 else None))
                    vmm = sPE.n
                    rec.op("scalar", lambda e, oc=oc, pb=pb:
                           e.activation(gate_t[:, oc * TW:(oc + 1) * TW], pb[:, 0:TW],
                                        AF.Silu, bias=col("bg", oc)),
                           waits=[(sPE, vmm)], inc=sACT)
                    psum_war[i] = sACT.n
                rec.op("gpsimd", lambda e, tt=tt: e.dma_start(
                    gv[:, :, tt * TW:(tt + 1) * TW],
                    gate_t[:, :].rearrange("p (c n) -> p c n", n=TW)),
                    waits=[(sACT, sACT.n)], inc=sST, inc_k=16)
                st_gate[tt] = sST.n
        wbuf_war[pi % 2] = sPE.n
        if pi + 2 < len(projs):
            vw_all.append(proj_weight_load(projs[pi + 2][1], wbuf[pi % 2], wbuf_war[pi % 2]))

    rec.op("gpsimd", lambda e: e.dma_start(
        wo_sb[:, :].rearrange("p (k m) -> p k m", m=D),
        Wo[:, :].rearrange("(k p) m -> p k m", p=128)),
        inc=sWA, inc_k=16)
    vwo = sWA.n
    quiesce("A12")
    regA12b.close()
    regA12.close()

    regA3 = contextlib.ExitStack()
    sbA3 = lambda name, shape, dt: regA3.enter_context(nc.sbuf_tensor(name, shape, dt))
    s_t = sbA3("s_t", [128, C * TW], F32)
    xorig_t = sbA3("xorig_t", [128, C * TW], F32)
    gg_sb = sbA3("gg_sb", [128, C * TW], BF16)
    sTa = sbA3("sTa", [128, 4 * TW], BF16)
    sTd = sbA3("sTd", [128, 4 * TW], BF16)
    ret_bf = sbA3("ret_bf", [128, TW], BF16)
    retsq_bf = sbA3("retsq_bf", [128, TW], BF16)

    sv = dview(s_d)
    warB = {0: (sACT, 0), 1: (sACT, 0), 3: (sACT, 0), 4: (sACT, 0)}
    war_slotA = [0, 0, 0, 0]
    war_slotD = [0, 0, 0, 0]
    ev_gg = 0
    st_s_val = 0
    v_apply_prev = 0
    for qt in range(NTT):
        rec.op("sync", lambda e, qt=qt: e.dma_start(
            gate_t[:, :].rearrange("p (c n) -> p c n", n=TW), gv[:, :, qt * TW:(qt + 1) * TW]),
            waits=[(sDVE, ev_gg), (sST, st_gate[NTT - 1])], inc=sLY, inc_k=16)
        vgate = sLY.n
        rec.op("sync", lambda e, qt=qt: e.dma_start(
            xorig_t[:, :].rearrange("p (c n) -> p c n", n=TW), xv[:, :, qt * TW:(qt + 1) * TW]),
            waits=[(sDVE, sDVE.n)], inc=sLY, inc_k=16)
        vxorig = sLY.n
        rec.op("sync", lambda e, qt=qt: e.dma_start(
            q_t[:, :].rearrange("p (c n) -> p c n", n=TW), qv[:, :, qt * TW:(qt + 1) * TW]),
            waits=[(sPE, sPE.n)], inc=sLY, inc_k=16)
        vq = sLY.n

        nkt = 4 * qt + 4
        for hp in range(H // 2):
            c = hp
            banks = [[0, 1], [3, 4]]
            na = [0, 0]
            nd = [0, 0]
            ev_info = [dict(), dict()]
            def emit_sc(hi, kt):
                rb = hi * 64
                h = 2 * hp + hi
                bid = banks[hi][kt % 2]
                pb = P[bid]
                ws, wv = warB[bid]
                w = [(ws, wv)]
                if kt == 0:
                    w.append((sLY, vq))
                rec.op("tensor", lambda e, c=c, rb=rb, kt=kt, pb=pb:
                       e.matmul(pb[:, 0:TW],
                                k_sb[rb:rb + 64, c * S + kt * 128: c * S + kt * 128 + 128],
                                q_t[rb:rb + 64, c * TW:(c + 1) * TW],
                                start=True, stop=True),
                       waits=w, inc=sPE)
                return sPE.n, bid

            def emit_evac(hi, kt, vsc, bid):
                pb = P[bid]
                masked = (kt >= 4 * qt)
                if not masked:
                    sl = hi * 2 + (na[hi] % 2)
                    if hi == 0:
                        rec.op("scalar", lambda e, pb=pb, sl=sl:
                               e.activation(sTa[:, sl * TW:(sl + 1) * TW], pb[:, 0:TW], AF.Copy),
                               waits=[(sPE, vsc), (sPE, war_slotA[sl])], inc=sACT)
                        warB[bid] = (sACT, sACT.n)
                        ev_info[hi][kt] = (sACT, sACT.n, "a", sl)
                    else:
                        rec.op("vector", lambda e, pb=pb, sl=sl:
                               e.tensor_copy(sTa[:, sl * TW:(sl + 1) * TW], pb[:, 0:TW]),
                               waits=[(sPE, vsc), (sPE, war_slotA[sl])], inc=sDVE)
                        warB[bid] = (sDVE, sDVE.n)
                        ev_info[hi][kt] = (sDVE, sDVE.n, "a", sl)
                    na[hi] += 1
                else:
                    j = kt - 4 * qt
                    sl = hi * 2 + (nd[hi] % 2)
                    rec.op("vector", lambda e, pb=pb, sl=sl, j=j:
                           e.tensor_tensor(sTd[:, sl * TW:(sl + 1) * TW], pb[:, 0:TW],
                                           masks_bf[:, j * TW:(j + 1) * TW], ALU.mult),
                           waits=[(sPE, vsc), (sPE, war_slotD[sl])], inc=sDVE)
                    warB[bid] = (sDVE, sDVE.n)
                    ev_info[hi][kt] = (sDVE, sDVE.n, "d", sl)
                    nd[hi] += 1

            def emit_ret(hi, kt):
                rb = hi * 64
                h = 2 * hp + hi
                evsem, vev, kind, sl = ev_info[hi][kt]
                buf = sTa if kind == "a" else sTd
                w = [(evsem, vev)]
                if kt == 0:
                    w.append((sDVE, v_apply_prev))
                rec.op("tensor", lambda e, kt=kt, h=h, rb=rb, buf=buf, sl=sl, nkt=nkt:
                       e.matmul(P[2][rb:rb + 64, 0:TW],
                                v_sb[:, kt * D + h * 64: kt * D + h * 64 + 64],
                                buf[:, sl * TW:(sl + 1) * TW],
                                start=(kt == 0), stop=(kt == nkt - 1)),
                       waits=w, inc=sPE)
                if kind == "a":
                    war_slotA[sl] = sPE.n
                else:
                    war_slotD[sl] = sPE.n
                return sPE.n

            vsc_bid = [None, None]
            for kt in range(nkt):
                for hi in range(2):
                    vsc_bid[hi] = emit_sc(hi, kt)
                if kt > 1:
                    for hi in range(2):
                        emit_ret(hi, kt - 2)
                for hi in range(2):
                    emit_evac(hi, kt, *vsc_bid[hi])
            for kt in (nkt - 2, nkt - 1):
                if kt >= 0:
                    for hi in range(2):
                        vret_last = emit_ret(hi, kt)
            rec.op("scalar", lambda e: e.activation(ret_bf[:, :], P[2][:, 0:TW], AF.Copy),
                   waits=[(sPE, vret_last), (sDVE, v_apply_prev)])
            rec.op("scalar", lambda e: e.activation(retsq_bf[:, :], ret_bf[:, :], AF.Square),
                   inc=sACT)
            vrsq = sACT.n
            rec.op("tensor", lambda e: e.matmul(P[5][0:2, 0:TW], selT_bf[:, 0:2],
                                                ret_bf[:, :], start=True, stop=True),
                   waits=[(sACT, vrsq)])
            rec.op("tensor", lambda e: e.matmul(P[6][0:2, 0:TW], selT_bf[:, 0:2],
                                                retsq_bf[:, :], start=True, stop=True),
                   inc=sPE)
            vrows = ln_rows(1.0 / 64.0, P[5], P[6], sPE.n, nr=2)
            rec.op("tensor", lambda e: e.matmul(P[7][:, 0:TW], sel2_bf[0:2, :],
                                                r_rbf[0:2, :], start=True, stop=True),
                   waits=[(sDVE, vrows)])
            rec.op("tensor", lambda e: e.matmul(P[2][:, 0:TW], sel2_bf[0:2, :],
                                                r_ubf[0:2, :], start=True, stop=True),
                   inc=sPE)
            vbc = sPE.n
            tm = tmp[2]
            rec.op("vector", lambda e, tm=tm: e.tensor_tensor(
                tm[:, :], ret_bf[:, :], P[7][:, 0:TW], ALU.mult),
                waits=[(sPE, vbc)])
            rec.op("vector", lambda e, tm=tm: e.tensor_tensor(
                tm[:, :], tm[:, :], P[2][:, 0:TW], ALU.subtract))
            rec.op("vector", lambda e, tm=tm, c=c: e.tensor_scalar(
                tm[:, :], tm[:, :], col("gnw", c), col("gnb", c), ALU.mult, ALU.add))
            rec.op("vector", lambda e, tm=tm, c=c: e.tensor_tensor(
                gg_sb[:, c * TW:(c + 1) * TW], tm[:, :],
                gate_t[:, c * TW:(c + 1) * TW], ALU.mult),
                waits=[(sLY, vgate)], inc=sDVE)
            v_apply_prev = sDVE.n
        ev_gg = sDVE.n
        for oc in range(C):
            i = oc % 2
            pb = P[i]
            ws, wv = warB[i]
            for c in range(C):
                w = [(ws, wv), (sDVE, ev_gg), (sWA, vwo)] if c == 0 else []
                rec.op("tensor", lambda e, c=c, oc=oc, pb=pb:
                       e.matmul(pb[:, 0:TW], wo_sb[:, c * D + oc * 128: c * D + oc * 128 + 128],
                                gg_sb[:, c * TW:(c + 1) * TW],
                                start=(c == 0), stop=(c == C - 1)),
                       waits=w, inc=(sPE if c == C - 1 else None))
            vmm = sPE.n
            rec.op("scalar", lambda e, oc=oc, pb=pb, i=i:
                   e.activation(tmp[i][:, :], pb[:, 0:TW], AF.Identity, bias=col("bo", oc)),
                   waits=[(sPE, vmm), (sDVE, ev_tmp[i]), (sST, st_s_val)], inc=sACT)
            warB[i] = (sACT, sACT.n)
            vev = sACT.n
            rec.op("vector", lambda e, oc=oc, i=i: e.tensor_tensor(
                s_t[:, oc * TW:(oc + 1) * TW], tmp[i][:, :],
                xorig_t[:, oc * TW:(oc + 1) * TW], ALU.add),
                waits=[(sACT, vev), (sLY, vxorig)], inc=sDVE)
            ev_tmp[i] = sDVE.n
        rec.op("gpsimd", lambda e, qt=qt: e.dma_start(
            sv[:, :, qt * TW:(qt + 1) * TW], s_t[:, :].rearrange("p (c n) -> p c n", n=TW)),
            waits=[(sDVE, sDVE.n)], inc=sST, inc_k=16)
        st_s_val = sST.n
    quiesce("A3")
    regA3.close()
    regSH.close()

    regA4 = contextlib.ExitStack()
    sbA4 = lambda name, shape, dt: regA4.enter_context(nc.sbuf_tensor(name, shape, dt))
    hidden = sbA4("hidden", [128, HC * S], BF16)

    regA4a = contextlib.ExitStack()
    sbA4a = lambda name, shape, dt: regA4a.enter_context(nc.sbuf_tensor(name, shape, dt))
    x2_sb = sbA4a("x2_sb", [128, C * S], BF16)
    uA = [sbA4a(f"uA{i}", [128, C * 128], BF16) for i in range(2)]
    uG = [sbA4a(f"uG{i}", [128, C * 128], BF16) for i in range(2)]
    s4_t = sbA4a("s4_t", [128, C * TW], F32)
    xbf4 = sbA4a("xbf4", [128, C * TW], BF16)
    sqbf4 = sbA4a("sqbf4", [128, C * TW], BF16)

    for tt in range(NTT):
        rec.op("sync", lambda e, tt=tt: e.dma_start(
            s4_t[:, :].rearrange("p (c n) -> p c n", n=TW), sv[:, :, tt * TW:(tt + 1) * TW]),
            waits=[(sDVE, sDVE.n)], inc=sLY, inc_k=16)
        vload = sLY.n
        vx2done = ln_tile(lambda c: s4_t[:, c * TW:(c + 1) * TW], ("n2w", "n2b"),
                          [(lambda c, tt=tt: x2_sb[:, c * S + tt * TW: c * S + (tt + 1) * TW], "aff")],
                          xbf4, sqbf4, src_ready=[(sLY, vload)])
        ev_tmp[0] = ev_tmp[1] = vx2done

    for oc in range(C):
        rec.op("gpsimd", lambda e, oc=oc: e.dma_start(
            woutbf_d[oc].rearrange("p (h m) -> p h m", m=128), woutP[oc]),
            inc=sST, inc_k=16)
    st_woutbf = sST.n

    war_ring = [0, 0]
    war_pu = [0, 0, 0, 0]
    ev_tmpA = [0, 0, 0, 0]
    for hc in range(HC):
        i = hc % 2
        rec.op("gpsimd", lambda e, hc=hc, i=i: e.dma_start(
            uA[i][:, :].rearrange("p (k m) -> p k m", m=128), winU[hc]),
            waits=[(sPE, war_ring[i])], inc=sWA, inc_k=16)
        rec.op("gpsimd", lambda e, hc=hc, i=i: e.dma_start(
            uG[i][:, :].rearrange("p (k m) -> p k m", m=128), winG[hc]),
            inc=sWA, inc_k=16)
        vw = sWA.n
        for tt in range(NTT):
            j = tt % 2
            pu = P[0 + j]
            pg = P[2 + j]
            for c in range(C):
                w = [(sWA, vw), (sACT, war_pu[j]), (sDVE, vx2done)] if c == 0 else []
                rec.op("tensor", lambda e, c=c, tt=tt, i=i, pu=pu:
                       e.matmul(pu[:, 0:TW], uA[i][:, c * 128:(c + 1) * 128],
                                x2_sb[:, c * S + tt * TW: c * S + (tt + 1) * TW],
                                start=(c == 0), stop=(c == C - 1)),
                       waits=w)
            for c in range(C):
                w = [(sACT, war_pu[2 + j]), (sDVE, vx2done)] if c == 0 else []
                rec.op("tensor", lambda e, c=c, tt=tt, i=i, pg=pg:
                       e.matmul(pg[:, 0:TW], uG[i][:, c * 128:(c + 1) * 128],
                                x2_sb[:, c * S + tt * TW: c * S + (tt + 1) * TW],
                                start=(c == 0), stop=(c == C - 1)),
                       waits=w, inc=(sPE if c == C - 1 else None))
            vmm = sPE.n
            rec.op("scalar", lambda e, hc=hc, j=j, pu=pu:
                   e.activation(tmp[j][:, :], pu[:, 0:TW], AF.Silu, bias=col("binu", hc)),
                   waits=[(sPE, vmm), (sDVE, ev_tmpA[j])], inc=sACT)
            war_pu[j] = sACT.n
            rec.op("scalar", lambda e, hc=hc, j=j, pg=pg:
                   e.activation(tmp[2 + j][:, :], pg[:, 0:TW], AF.Identity, bias=col("bing", hc)),
                   waits=[(sDVE, ev_tmpA[2 + j])], inc=sACT)
            war_pu[2 + j] = sACT.n
            vev = sACT.n
            rec.op("vector", lambda e, hc=hc, tt=tt, j=j: e.tensor_tensor(
                hidden[:, hc * S + tt * TW: hc * S + (tt + 1) * TW],
                tmp[j][:, :], tmp[2 + j][:, :], ALU.mult),
                waits=[(sACT, vev)], inc=sDVE)
            ev_tmpA[j] = sDVE.n
            ev_tmpA[2 + j] = sDVE.n
        war_ring[i] = sPE.n
    quiesce("A4a")
    regA4a.close()

    regA4b = contextlib.ExitStack()
    sbA4b = lambda name, shape, dt: regA4b.enter_context(nc.sbuf_tensor(name, shape, dt))
    wring = [sbA4b(f"wring{i}", [128, HC * 128], BF16) for i in range(2)]
    s_t2 = sbA4b("s_t2", [128, C * TW], F32)
    xo_t2 = sbA4b("xo_t2", [128, C * TW], F32)
    x3bf_t = sbA4b("x3bf_t", [128, C * TW], BF16)
    xbf5 = sbA4b("xbf5", [128, C * TW], BF16)
    sqbf5 = sbA4b("sqbf5", [128, C * TW], BF16)

    x3v = dview(x3_d)
    x3bv = dview(x3bf_d)
    war_wring = [0, 0]
    war_po = [0, 0]
    war_rps = [0, 0]
    st_x3 = 0
    for tt in range(NTT):
        rec.op("sync", lambda e, tt=tt: e.dma_start(
            s_t2[:, :].rearrange("p (c n) -> p c n", n=TW), sv[:, :, tt * TW:(tt + 1) * TW]),
            waits=[(sDVE, sDVE.n), (sST, st_x3)], inc=sLY, inc_k=16)
        vs = sLY.n
        rec.op("sync", lambda e, tt=tt: e.dma_start(
            xo_t2[:, :].rearrange("p (c n) -> p c n", n=TW), xv[:, :, tt * TW:(tt + 1) * TW]),
            waits=[(sDVE, sDVE.n)], inc=sLY, inc_k=16)
        vxo = sLY.n
        for oc in range(C):
            i = oc % 2
            pb = P[4 + i]
            rec.op("sync", lambda e, oc=oc, i=i: e.dma_start(
                wring[i][:, :], woutbf_d[oc]),
                waits=[(sPE, war_wring[i]), (sST, st_woutbf)], inc=sLX, inc_k=16)
            vw = sLX.n
            for hc in range(HC):
                w = [(sLX, vw), (sACT, war_po[i])] if hc == 0 else []
                rec.op("tensor", lambda e, hc=hc, tt=tt, i=i, pb=pb:
                       e.matmul(pb[:, 0:TW], wring[i][:, hc * 128:(hc + 1) * 128],
                                hidden[:, hc * S + tt * TW: hc * S + (tt + 1) * TW],
                                start=(hc == 0), stop=(hc == HC - 1)),
                       waits=w, inc=(sPE if hc == HC - 1 else None))
            war_wring[i] = sPE.n
            vmm = sPE.n
            rec.op("scalar", lambda e, oc=oc, pb=pb, i=i:
                   e.activation(tmp[i][:, :], pb[:, 0:TW], AF.Identity, bias=col("bout", oc)),
                   waits=[(sPE, vmm), (sDVE, ev_tmp[i])], inc=sACT)
            war_po[i] = sACT.n
            vev = sACT.n
            rec.op("vector", lambda e, oc=oc, i=i: e.tensor_tensor(
                s_t2[:, oc * TW:(oc + 1) * TW], s_t2[:, oc * TW:(oc + 1) * TW],
                tmp[i][:, :], ALU.add),
                waits=[(sACT, vev), (sLY, vs)])
            rec.op("vector", lambda e, oc=oc: e.tensor_tensor(
                xo_t2[:, oc * TW:(oc + 1) * TW], xo_t2[:, oc * TW:(oc + 1) * TW],
                s_t2[:, oc * TW:(oc + 1) * TW], ALU.add),
                waits=[(sLY, vxo)] if oc == 0 else (), inc=sDVE)
            ev_tmp[i] = sDVE.n
        vdone = ln_tile(lambda c: xo_t2[:, c * TW:(c + 1) * TW], None,
                        [(lambda c: s_t2[:, c * TW:(c + 1) * TW], "f32"),
                         (lambda c: x3bf_t[:, c * TW:(c + 1) * TW], "bf16")],
                        xbf5, sqbf5, src_ready=[(sST, st_x3)])
        ev_tmp[0] = ev_tmp[1] = vdone
        rec.op("gpsimd", lambda e, tt=tt: e.dma_start(
            x3v[:, :, tt * TW:(tt + 1) * TW], s_t2[:, :].rearrange("p (c n) -> p c n", n=TW)),
            waits=[(sDVE, vdone)], inc=sST, inc_k=16)
        rec.op("gpsimd", lambda e, tt=tt: e.dma_start(
            x3bv[:, :, tt * TW:(tt + 1) * TW], x3bf_t[:, :].rearrange("p (c n) -> p c n", n=TW)),
            inc=sST, inc_k=16)
        st_x3 = sST.n
        for grp in range(4):
            i = grp % 2
            pb = P[6 + i]
            for c in range(C):
                w = [(sDVE, vdone), (sDVE, war_rps[i])] if c == 0 else []
                rec.op("tensor", lambda e, c=c, grp=grp, pb=pb:
                       e.matmul(pb[0:128, 0:NEX],
                                x3bf_t[:, c * TW + grp * 128: c * TW + grp * 128 + 128],
                                gateWT_bf[:, c * NEX:(c + 1) * NEX],
                                start=(c == 0), stop=(c == C - 1)),
                       waits=w, inc=(sPE if c == C - 1 else None))
            vmm = sPE.n
            vop(lambda e, pb=pb: e.tensor_copy(g_tm[:, :], pb[0:128, 0:NEX]),
                extra=[(sPE, vmm), (sACT, ev_router_act[0])])
            vop(lambda e: e.reduce_max(cc[:, 2:3], g_tm[:, :], axis=AX.X))
            vop(lambda e: e.tensor_single_scalar(g_t2[:, :], g_tm[:, :], cc[:, 2:3], ALU.subtract))
            vop(lambda e: e.tensor_single_scalar(g_t2[:, :], g_t2[:, :], 0.0, ALU.is_ge))
            vop(lambda e: e.scalar_tensor_tensor(g_t1[:, :], g_t2[:, :], -1e30, g_tm[:, :], ALU.mult, ALU.add))
            vop(lambda e: e.reduce_max(cc[:, 3:4], g_t1[:, :], axis=AX.X))
            vop(lambda e: e.tensor_scalar_mul(cc[:, 4:5], cc[:, 2:3], -1.0))
            vd = vop(lambda e: e.tensor_tensor(cc[:, 5:6], cc[:, 2:3], cc[:, 3:4], ALU.subtract))
            vop(lambda e: e.activation(g_t1[:, :], g_tm[:, :], AF.Exp, bias=cc[:, 4:5]),
                extra=[(sDVE, vd)], engine="scalar")
            va = vop(lambda e: e.activation(cc[:, 6:7], cc[:, 5:6], AF.Sigmoid),
                     engine="scalar")
            ev_router_act[0] = va
            vop(lambda e: e.tensor_single_scalar(g_t2[:, :], g_tm[:, :], cc[:, 3:4], ALU.is_ge),
                extra=[(sACT, va)])
            vop(lambda e: e.tensor_tensor(g_tm[:, :], g_t1[:, :], g_t2[:, :], ALU.mult))
            vg = vop(lambda e: e.tensor_scalar_mul(g_tm[:, :], g_tm[:, :], cc[:, 6:7]))
            rec.op("tensor", lambda e, pb=pb: e.transpose(pb[0:NEX, 0:128], g_tm[:, :], ident[:, :]),
                   waits=[(sDVE, vg)], inc=sPE)
            vtr = sPE.n
            rec.op("vector", lambda e, tt=tt, grp=grp, pb=pb: e.tensor_copy(
                g_fm[0:NEX, tt * TW + grp * 128: tt * TW + grp * 128 + 128], pb[0:NEX, 0:128]),
                waits=[(sPE, vtr)], inc=sDVE)
            war_rps[i] = sDVE.n
    quiesce("A4b")
    regA4b.close()
    regA4.close()
    regAW.close()

    regB1 = contextlib.ExitStack()
    sbB1 = lambda name, shape, dt: regB1.enter_context(nc.sbuf_tensor(name, shape, dt))
    W1sb = sbB1("W1sb", [128, K2 * C * 128], BF16)
    W2sb = sbB1("W2sb", [128, C * K2 * 128], BF16)
    sh_sb = sbB1("sh_sb", [128, K2 * TW], BF16)
    x3bf_b = sbB1("x3bf_b", [128, C * TW], BF16)
    acc_t = [sbB1(f"acc_t{i}", [128, TW], F32) for i in range(2)]

    accv = dview(acc_d)
    war_x3bf = 0
    war_sh = 0
    war_ph = [0, 0]
    war_po2 = [0, 0]
    war_ge = 0
    ev_acc = [0, 0]
    st_acc = [[0] * C for _ in range(NTT)]
    st_buf = [0, 0]

    def w1_quarter(E, qtr, wait_pe):
        if E < 2:
            rec.op("gpsimd", lambda e, E=E, qtr=qtr: e.dma_start(
                W1sb[:, qtr * 8192:(qtr + 1) * 8192].rearrange("p (a k m) -> p a k m", a=8, k=C),
                exp1[E, qtr * 8: (qtr + 1) * 8].rearrange("a p k m -> p a k m")),
                waits=[(sPE, wait_pe)], inc=sW1, inc_k=16)
        else:
            rec.op("sync", lambda e, E=E, qtr=qtr: e.dma_start(
                W1sb[:, qtr * 8192:(qtr + 1) * 8192].rearrange("p (a f) -> p a f", a=8),
                exp1bf_d[E, qtr * 8: (qtr + 1) * 8].rearrange("a p f -> p a f")),
                waits=[(sPE, wait_pe), (sST, cast_w1_done[E])], inc=sW1, inc_k=16)

    def w2_quarter(E, qtr, wait_pe):
        if E < 2:
            rec.op("gpsimd", lambda e, E=E, qtr=qtr: e.dma_start(
                W2sb[:, qtr * 8192:(qtr + 1) * 8192].rearrange("p (o k m) -> p o k m", o=2, k=K2),
                exp2[E, qtr * 2: (qtr + 1) * 2].rearrange("o p k m -> p o k m")),
                waits=[(sPE, wait_pe)], inc=sW2, inc_k=16)
        else:
            rec.op("sync", lambda e, E=E, qtr=qtr: e.dma_start(
                W2sb[:, qtr * 8192:(qtr + 1) * 8192].rearrange("p (o f) -> p o f", o=2),
                exp2bf_d[E, qtr * 2: (qtr + 1) * 2].rearrange("o p f -> p o f")),
                waits=[(sPE, wait_pe), (sST, cast_w2_done[E])], inc=sW2, inc_k=16)

    for qtr in range(4):
        w1_quarter(0, qtr, 0)
    vW1 = sW1.n
    for qtr in range(4):
        w2_quarter(0, qtr, 0)
    vW2 = sW2.n
    for E in range(NE if NE_RUN is None else NE_RUN):
        vW1_next = vW1
        vW2_next = vW2
        for tt in range(NTT):
            if E + 2 < NE:
                drip_cast(1)
            rec.op("sync", lambda e, tt=tt: e.dma_start(
                x3bf_b[:, :].rearrange("p (c n) -> p c n", n=TW), x3bv[:, :, tt * TW:(tt + 1) * TW]),
                waits=[(sPE, war_x3bf), (sST, st_x3)], inc=sLX, inc_k=16)
            vx3 = sLX.n
            if E > 0:
                rec.op("tensor", lambda e, E=E, tt=tt: e.matmul(
                    P[4][:, 0:TW], esel_bf[0:NEX, (E - 1) * 128: E * 128],
                    g_fm[0:NEX, tt * TW:(tt + 1) * TW], start=True, stop=True),
                    waits=[(sDVE, war_ge)], inc=sPE)
                vge = sPE.n
                rec.op("vector", lambda e: e.tensor_copy(Ge_sb[:, :], P[4][:, 0:TW]),
                       waits=[(sPE, vge)], inc=sDVE)
                war_ge = sDVE.n
                vge_d = sDVE.n
            else:
                vge_d = 0
            if ge_tap is not None and E == 1:
                rec.op("gpsimd", lambda e, tt=tt: e.dma_start(
                    ge_tap[:, tt * TW:(tt + 1) * TW], Ge_sb[:, :]),
                    waits=[(sDVE, vge_d)], inc=sST, inc_k=16)
            for k2 in range(K2):
                i = k2 % 2
                pb = P[i]
                for c in range(C):
                    w = []
                    if c == 0:
                        w = [(sLX, vx3), (sW1, vW1), (sACT, war_ph[i]), (sPE, war_sh)]
                    rec.op("tensor", lambda e, c=c, k2=k2, pb=pb:
                           e.matmul(pb[:, 0:TW], W1sb[:, k2 * 1024 + c * 128: k2 * 1024 + c * 128 + 128],
                                    x3bf_b[:, c * TW:(c + 1) * TW],
                                    start=(c == 0), stop=(c == C - 1)),
                           waits=w, inc=(sPE if c == C - 1 else None))
                vmm = sPE.n
                rec.op("scalar", lambda e, k2=k2, pb=pb:
                       e.activation(sh_sb[:, k2 * TW:(k2 + 1) * TW], pb[:, 0:TW], AF.Silu),
                       waits=[(sPE, vmm)], inc=sACT)
                war_ph[i] = sACT.n
                if tt == NTT - 1 and E + 1 < NE and (k2 % 8) == 7:
                    w1_quarter(E + 1, k2 // 8, sPE.n)
            if tt == NTT - 1 and E + 1 < NE:
                vW1_next = sW1.n
            vsilu = sACT.n
            war_x3bf = sPE.n
            for oc in range(C):
                i = oc % 2
                pb = P[2 + i]
                if E > 0:
                    rec.op("sync", lambda e, tt=tt, oc=oc, i=i: e.dma_start(
                        acc_t[i][:, :].rearrange("p (a n) -> p a n", a=1),
                        accv[:, oc:oc + 1, tt * TW:(tt + 1) * TW]),
                        waits=[(sDVE, ev_acc[i]),
                               (sST, max(st_acc[tt][oc], st_buf[i]))],
                        inc=sLZ, inc_k=16)
                    vacc = sLZ.n
                for k2 in range(K2):
                    w = []
                    if k2 == 0:
                        w = [(sW2, vW2), (sACT, vsilu), (sDVE, war_po2[i])]
                    rec.op("tensor", lambda e, k2=k2, oc=oc, pb=pb:
                           e.matmul(pb[:, 0:TW], W2sb[:, oc * 4096 + k2 * 128: oc * 4096 + k2 * 128 + 128],
                                    sh_sb[:, k2 * TW:(k2 + 1) * TW],
                                    start=(k2 == 0), stop=(k2 == K2 - 1)),
                           waits=w, inc=(sPE if k2 == K2 - 1 else None))
                vmm = sPE.n
                if E == 0:
                    rec.op("vector", lambda e, i=i, pb=pb: e.tensor_copy(acc_t[i][:, :], pb[:, 0:TW]),
                           waits=[(sPE, vmm), (sST, st_buf[i])], inc=sDVE)
                else:
                    rec.op("vector", lambda e, i=i, pb=pb: e.tensor_tensor(
                        tmp[2 + i][:, :], pb[:, 0:TW], Ge_sb[:, :], ALU.mult),
                        waits=[(sPE, vmm), (sDVE, vge_d), (sLZ, vacc)])
                    rec.op("vector", lambda e, i=i: e.tensor_tensor(
                        acc_t[i][:, :], acc_t[i][:, :], tmp[2 + i][:, :], ALU.add),
                        inc=sDVE)
                war_po2[i] = sDVE.n
                ev_acc[i] = sDVE.n
                vadd = sDVE.n
                rec.op("gpsimd", lambda e, tt=tt, oc=oc, i=i: e.dma_start(
                    accv[:, oc:oc + 1, tt * TW:(tt + 1) * TW],
                    acc_t[i][:, :].rearrange("p (a n) -> p a n", a=1)),
                    waits=[(sDVE, vadd)], inc=sST, inc_k=16)
                st_acc[tt][oc] = sST.n
                st_buf[i] = sST.n
                if tt == NTT - 1 and E + 1 < NE and (oc % 2) == 1:
                    w2_quarter(E + 1, oc // 2, sPE.n)
            if tt == NTT - 1 and E + 1 < NE:
                vW2_next = sW2.n
            war_sh = sPE.n
        vW1 = vW1_next
        vW2 = vW2_next
    quiesce("B1")
    regB1.close()

    regB2 = contextlib.ExitStack()
    sbB2 = lambda name, shape, dt: regB2.enter_context(nc.sbuf_tensor(name, shape, dt))
    accB = [sbB2(f"accB{i}", [128, C * TW], F32) for i in range(2)]
    x3B = [sbB2(f"x3B{i}", [128, C * TW], F32) for i in range(2)]
    z2B = sbB2("z2B", [128, C * TW], F32)
    outB = sbB2("outB", [128, C * TW], F32)
    xbf6 = sbB2("xbf6", [128, C * TW], BF16)
    sqbf6 = sbB2("sqbf6", [128, C * TW], BF16)

    outv = dview(out_d)
    st_out = 0
    vldB = [0] * NTT
    zwar = [0, 0]

    def b2_load(tt):
        rec.op("sync", lambda e, tt=tt: e.dma_start(
            accB[tt % 2][:, :].rearrange("p (c n) -> p c n", n=TW), accv[:, :, tt * TW:(tt + 1) * TW]),
            waits=[(sDVE, zwar[tt % 2])], inc=sLX, inc_k=16)
        rec.op("sync", lambda e, tt=tt: e.dma_start(
            x3B[tt % 2][:, :].rearrange("p (c n) -> p c n", n=TW), x3v[:, :, tt * TW:(tt + 1) * TW]),
            inc=sLX, inc_k=16)
        vldB[tt] = sLX.n

    b2_load(0)
    b2_load(1)
    for tt in range(NTT):
        for c in range(C):
            w = [(sLX, vldB[tt])] if c == 0 else []
            rec.op("vector", lambda e, c=c, tt=tt: e.tensor_tensor(
                z2B[:, c * TW:(c + 1) * TW], x3B[tt % 2][:, c * TW:(c + 1) * TW],
                accB[tt % 2][:, c * TW:(c + 1) * TW], ALU.add),
                waits=w, inc=(sDVE if c == C - 1 else None))
        zwar[tt % 2] = sDVE.n
        if tt + 2 < NTT:
            b2_load(tt + 2)
        vz = sDVE.n
        vdone = ln_tile(lambda c: z2B[:, c * TW:(c + 1) * TW], None,
                        [(lambda c: outB[:, c * TW:(c + 1) * TW], "f32")],
                        xbf6, sqbf6, src_ready=[(sDVE, vz)])
        rec.op("gpsimd", lambda e, tt=tt: e.dma_start(
            outv[:, :, tt * TW:(tt + 1) * TW], outB[:, :].rearrange("p (c n) -> p c n", n=TW)),
            waits=[(sDVE, vdone), (sST, st_out)], inc=sST, inc_k=16)
        st_out = sST.n

    if gfm_tap is not None:
        rec.op("gpsimd", lambda e: e.dma_start(gfm_tap[:, :], g_fm[0:NEX, :]),
               waits=[(sST, sST.n)], inc=sST, inc_k=16)
    for (tap_d, src_d) in tap_outs:
        rec.op("gpsimd", lambda e, tap_d=tap_d, src_d=src_d: e.dma_start(
            tap_d[:, :], src_d[:, :]), waits=[(sST, sST.n)], inc=sST, inc_k=16)

    quiesce("B2")

    with nc.Block() as block:
        block.gpsimd(lambda e: e.wait_ge(sST.h, sST.n))

    return nc


def _pack_inputs(inputs):
    f = np.float32
    A = lambda v: np.ascontiguousarray(np.asarray(v, dtype=f))

    x = A(inputs["L0_attn_x"])
    Wq = A(inputs["L0_attn_wq"]); Wk = A(inputs["L0_attn_wk"])
    Wv = A(inputs["L0_attn_wv"]); Wg = A(inputs["L0_attn_wg"])
    Wo = A(inputs["L0_attn_wo"])
    w_in = A(inputs["L0_attn_w_in"])
    w_out = A(inputs["L0_attn_w_out"])
    sh1 = A(inputs["L0_ffn_shared_w1"])
    sh2 = A(inputs["L0_ffn_shared_w2"])
    ew1 = A(inputs["L0_ffn_experts_w1"])
    ew2 = A(inputs["L0_ffn_experts_w2"])
    gw = A(inputs["L0_ffn_gate_w"])

    wu = np.zeros((D, HIDP), f); wu[:, :HID] = w_in[:, :HID]
    wgg = np.zeros((D, HIDP), f); wgg[:, :HID] = w_in[:, HID:]
    winU = np.ascontiguousarray(wu.reshape(C, 128, HC, 128).transpose(2, 1, 0, 3))
    winG = np.ascontiguousarray(wgg.reshape(C, 128, HC, 128).transpose(2, 1, 0, 3))
    wop = np.zeros((HIDP, D), f); wop[:HID] = w_out
    woutP = np.ascontiguousarray(wop.reshape(HC, 128, C, 128).transpose(2, 1, 0, 3))

    e1 = np.empty((NE, K2, 128, C, 128), f)
    e2 = np.empty((NE, C, 128, K2, 128), f)
    for E in range(NE):
        w1 = sh1 if E == 0 else ew1[E - 1]
        w2 = sh2 if E == 0 else ew2[E - 1]
        w1T = np.ascontiguousarray(w1.T)
        w2T = np.ascontiguousarray(w2.T)
        e1[E] = w1T.reshape(C, 128, K2, 128).transpose(2, 1, 0, 3)
        e2[E] = w2T.reshape(K2, 128, C, 128).transpose(2, 1, 0, 3)

    gateWT = np.ascontiguousarray(gw.T)

    pc = np.zeros((128, PCOLS_W), f)
    def setc(name, vec, n):
        pc[:, PCOL_OFF[name]:PCOL_OFF[name] + n] = np.asarray(vec, f).reshape(n, 128).T
    setc("n1w", inputs["L0_attn_norm1_w"], 8); setc("n1b", inputs["L0_attn_norm1_b"], 8)
    setc("n2w", inputs["L0_attn_norm2_w"], 8); setc("n2b", inputs["L0_attn_norm2_b"], 8)
    setc("gnw", inputs["L0_attn_gn_w"], 8); setc("gnb", inputs["L0_attn_gn_b"], 8)
    setc("bq", inputs["L0_attn_bq"], 8); setc("bk", inputs["L0_attn_bk"], 8)
    setc("bg", inputs["L0_attn_bg"], 8); setc("bo", inputs["L0_attn_bo"], 8)
    setc("bout", inputs["L0_attn_b_out"], 8)
    b_in = A(inputs["L0_attn_b_in"])
    biu = np.zeros(HIDP, f); biu[:HID] = b_in[:HID]
    big = np.zeros(HIDP, f); big[:HID] = b_in[HID:]
    setc("binu", biu, HC); setc("bing", big, HC)
    gam = 1.0 - 2.0 ** (-5.0 - np.arange(H, dtype=np.float64))
    lng = np.log(gam)
    pp, ccx = np.meshgrid(np.arange(128), np.arange(8), indexing="ij")
    hmap = 2 * ccx + pp // 64
    pc[:, PCOL_OFF["lng"]:PCOL_OFF["lng"] + 8] = lng[hmap].astype(f)
    pc[:, PCOL_OFF["nlng"]:PCOL_OFF["nlng"] + 8] = (-lng[hmap]).astype(f)

    m, n = np.meshgrid(np.arange(128), np.arange(TW), indexing="ij")
    masks = np.concatenate([(n - m >= 128 * j).astype(f) for j in range(4)], axis=1)
    masks = np.ascontiguousarray(masks)
    R2 = np.broadcast_to(np.arange(TW, dtype=f), (128, TW)).copy()
    ident = np.eye(128, dtype=f)
    esel = np.zeros((NEX, NEX * 128), f)
    for e in range(NEX):
        esel[e, e * 128:(e + 1) * 128] = 1.0
    bv_row = A(inputs["L0_attn_bv"]).reshape(1, D)
    sel2 = np.zeros((2, 128), f)
    sel2[0, 0:64] = 1.0
    sel2[1, 64:128] = 1.0
    selT = np.ascontiguousarray(sel2.T)

    shared = dict(Wq=Wq, Wk=Wk, Wv=Wv, Wg=Wg, Wo=Wo, winU=winU, winG=winG,
                  woutP=woutP, exp1=e1, exp2=e2, gateWT=gateWT, pcols_in=pc,
                  masks_in=masks, R2_in=R2, ident_in=ident, esel_in=esel, bv_row=bv_row,
                  sel2_in=sel2, selT_in=selT)
    in_maps = []
    for b in range(8):
        m_ = dict(shared)
        m_["x_in"] = np.ascontiguousarray(x[b].T)
        in_maps.append(m_)
    return in_maps


def kernel(**inputs):
    in_maps = _pack_inputs(inputs)
    nc = build_program()
    res = run_bass_kernel_spmd(nc, in_maps, core_ids=list(range(8)))
    out = np.stack([np.asarray(r["out"]).T for r in res.results], axis=0)
    return np.ascontiguousarray(out.astype(np.float32))
